# version 29 (final)
# baseline (speedup 1.0000x reference)
import numpy as np
from contextlib import ExitStack

import concourse.bass as bass
import concourse.mybir as mybir
from concourse.bass_utils import run_bass_kernel_spmd

F32 = mybir.dt.float32
BF16 = mybir.dt.bfloat16
ACT = mybir.ActivationFunctionType
ALU = mybir.AluOpType

D = 1024
DC = 8
BATCH = 4
L = 4096
TP = 2048
TC = 512
NTC = TP // TC
NST = 16
HID = 4096
HC = HID // 128
KER = 4

ENGINES = ("sync", "vector", "scalar", "tensor", "gpsimd")

DBG_TAPS = []
LIMIT_NTC = None
SKIP_P1 = False
STAGE_LIMIT = None


class Em:

    def __init__(self, nc, stk):
        self.nc = nc
        self.queues = {e: [] for e in ENGINES}
        self.sems = {}
        self.counts = {}
        for e in ENGINES:
            for kind in ("c", "d"):
                name = f"{e}_{kind}"
                self.sems[name] = stk.enter_context(nc.semaphore(name))
                self.counts[name] = 0
        for i in range(8):
            name = f"rg{i}"
            self.sems[name] = stk.enter_context(nc.semaphore(name))
            self.counts[name] = 0

    def wait(self, eng, *toks):
        best = {}
        for t in toks:
            if t is None:
                continue
            best[t[0]] = max(best.get(t[0], 0), t[1])
        for name, v in sorted(best.items()):
            if name == f"{eng}_c":
                continue
            sem = self.sems[name]
            self.queues[eng].append(lambda E, sem=sem, v=v: E.wait_ge(sem, v))

    def op(self, eng, fn, sig=False):
        if not sig:
            self.queues[eng].append(fn)
            return None
        name = f"{eng}_c"
        self.counts[name] += 1
        v = self.counts[name]
        sem = self.sems[name]

        def w(E, fn=fn, sem=sem):
            inst = fn(E)
            inst.then_inc(sem, 1)

        self.queues[eng].append(w)
        return (name, v)

    def sig_last(self, eng):
        return self.op(eng, lambda E: E.nop(), sig=True)

    def dma(self, eng, out, in_, serial=False, sem_name=None):
        name = sem_name or f"{eng}_d"
        prev = self.counts[name]
        if serial and prev:
            sem_p = self.sems[name]
            self.queues[eng].append(lambda E, s=sem_p, v=prev: E.wait_ge(s, v))
        self.counts[name] += 16
        v = self.counts[name]
        sem = self.sems[name]

        def w(E, out=out, in_=in_, sem=sem):
            E.dma_start(out=out, in_=in_).then_inc(sem, 16)

        self.queues[eng].append(w)
        return (name, v)

    def finalize(self, block):
        for e in ENGINES:
            dec = getattr(block, e)
            qq = self.queues[e]

            def body(E, qq=qq):
                for fn in qq:
                    fn(E)

            dec(body)


class Prog:
    def __init__(self, dbg_names=(), dbg_phase=2):
        self.stk = ExitStack()
        nc = bass.Bass("TRN2", target_bir_lowering=False, detect_race_conditions=False)
        self.nc = nc
        self.em = Em(nc, self.stk)
        self.pb = [self.stk.enter_context(nc.psum_tensor(f"pb{i}", [128, TC], F32))
                   for i in range(8)]
        self.pb_free = [None] * 8
        self.dbg_names = list(dbg_names)
        self.dbg_phase = dbg_phase
        self.out_toks = []
        self.chunk_guard = []

        d = {}
        dt_ = nc.dram_tensor
        d["x_pre"] = dt_("x_pre", [DC, 128, TP + 3], F32, kind="ExternalInput").ap()
        d["x_own"] = dt_("x_own", [DC, 128, TP + 3], F32, kind="ExternalInput").ap()
        d["flag"] = dt_("flag", [128, 1], F32, kind="ExternalInput").ap()
        d["wr"] = dt_("wr", [DC, 128, DC * 128], F32, kind="ExternalInput").ap()
        d["wi"] = dt_("wi", [DC, 128, DC * 128], F32, kind="ExternalInput").ap()
        d["wo0"] = dt_("wo0", [DC, 128, DC * 128], F32, kind="ExternalInput").ap()
        d["w1_0"] = dt_("w1_0", [HC, 128, DC * 128], F32, kind="ExternalInput").ap()
        d["w2_0"] = dt_("w2_0", [DC, 128, HC * 128], F32, kind="ExternalInput").ap()
        d["wip"] = dt_("wip", [2 * DC, 128, DC * 128], F32, kind="ExternalInput").ap()
        d["wop"] = dt_("wop", [DC, 128, DC * 128], F32, kind="ExternalInput").ap()
        d["w1_1"] = dt_("w1_1", [HC, 128, DC * 128], F32, kind="ExternalInput").ap()
        d["w2_1"] = dt_("w2_1", [DC, 128, HC * 128], F32, kind="ExternalInput").ap()
        d["wxp"] = dt_("wxp", [128, DC * 33], F32, kind="ExternalInput").ap()
        for nmv, cols in (("wr_b", 1), ("wi_b", 1), ("cw0", KER), ("cb0", 1),
                          ("log_a", 1), ("norm_w", 1), ("b2_0", 1),
                          ("cw1", KER), ("cb1", 1), ("dtw", 1), ("dtb", 1),
                          ("alog", NST), ("b2_1", 1)):
            d[nmv] = dt_(nmv, [DC, 128, cols], F32, kind="ExternalInput").ap()
        d["consts"] = dt_("consts", [128, 4], F32, kind="ExternalInput").ap()
        d["log_a_t"] = dt_("log_a_t", [128, DC], F32, kind="ExternalInput").ap()
        d["bigrow"] = dt_("bigrow", [1, 512], F32, kind="ExternalInput").ap()
        d["b1_0"] = dt_("b1_0", [128, HC], F32, kind="ExternalInput").ap()
        d["b1_1"] = dt_("b1_1", [128, HC], F32, kind="ExternalInput").ap()
        d["out"] = dt_("out", [DC, 128, TP], F32, kind="ExternalOutput").ap()
        d["ssm_hbm"] = dt_("ssm_hbm", [2, 33, TC], F32, kind="Internal").ap()
        self.wsplit = {}
        for wnm, mc_n, kc_n in (("wr", DC, DC), ("wi", DC, DC), ("wo0", DC, DC),
                                ("w1_0", HC, DC), ("w2_0", DC, HC),
                                ("wip", 2 * DC, DC), ("wop", DC, DC),
                                ("w1_1", HC, DC), ("w2_1", DC, HC)):
            hi = dt_(wnm + "_hi", [mc_n, 128, kc_n * 128], BF16, kind="Internal").ap()
            lo = dt_(wnm + "_lo", [mc_n, 128, kc_n * 128], BF16, kind="Internal").ap()
            self.wsplit[wnm] = (hi, lo, mc_n, kc_n)
        if self.dbg_names:
            d["dbg"] = dt_("dbg", [len(self.dbg_names), DC, 128, TP], F32,
                           kind="ExternalOutput").ap()
        self.d = d

    def sb(self, name, cols, parts=128):
        return self.stk.enter_context(self.nc.sbuf_tensor(name, [parts, cols], F32))

    def tap(self, name, tiles, tok, phase, tc):
        if name not in self.dbg_names or phase != self.dbg_phase:
            return None
        i = self.dbg_names.index(name)
        t = None
        for dc in range(DC):
            self.em.wait("gpsimd", tok)
            t = self.em.dma("gpsimd", self.d["dbg"][i, dc, :, tc * TC:(tc + 1) * TC],
                            tiles[dc][:, 0:TC])
            self.out_toks.append(t)
        return t

    def build(self):
        em = self.em
        nc = self.nc

        self.ones_col = self.sb("ones_col", 1)
        self.ones_row = self.stk.enter_context(nc.sbuf_tensor("ones_row", [1, 128], F32))
        self.row_m = self.stk.enter_context(nc.sbuf_tensor("row_m", [1, TC], F32))
        self.row_r = self.stk.enter_context(nc.sbuf_tensor("row_r", [1, TC], F32))
        self.row_t = self.stk.enter_context(nc.sbuf_tensor("row_t", [1, TC], F32))
        self.eps_ln = self.stk.enter_context(nc.sbuf_tensor("eps_ln", [1, 1], F32))
        self.eps_rms = self.stk.enter_context(nc.sbuf_tensor("eps_rms", [1, 1], F32))
        em.op("vector", lambda E: E.memset(self.ones_col[:], 1.0))
        em.op("vector", lambda E: E.memset(self.ones_row[:], 1.0))
        em.op("vector", lambda E: E.memset(self.eps_ln[:], 1e-5))
        self.init_tok = em.op("vector", lambda E: E.memset(self.eps_rms[:], 1e-6), sig=True)

        self.v = {}
        self.vtok = {}
        for nmv, cols in (("wr_b", 1), ("wi_b", 1), ("cw0", KER), ("cb0", 1),
                          ("log_a", 1), ("norm_w", 1), ("b2_0", 1),
                          ("cw1", KER), ("cb1", 1), ("dtw", 1), ("dtb", 1),
                          ("alog", NST), ("b2_1", 1)):
            t = self.sb("v_" + nmv, DC * cols)
            tk = None
            for dc in range(DC):
                tk = em.dma("sync", t[:, dc * cols:(dc + 1) * cols], self.d[nmv][dc])
            self.v[nmv] = t
            self.vtok[nmv] = tk
        for nmv in ("b1_0", "b1_1"):
            t = self.sb("v_" + nmv, HC)
            self.v[nmv] = t
            self.vtok[nmv] = em.dma("sync", t[:], self.d[nmv])
        self.flag = self.sb("flag_sb", 1)
        self.vtok["flag"] = em.dma("sync", self.flag[:], self.d["flag"])
        self.wxp = self.sb("wxp_sb", DC * 33)
        self.vtok["wxp"] = em.dma("sync", self.wxp[:], self.d["wxp"])
        self.vec_tok = self.vtok["flag"]

        self.Aneg = self.sb("Aneg", DC * NST)
        em.wait("scalar", self.vtok["alog"])
        self.A_tok0 = em.op("scalar", lambda E: E.activation(self.Aneg[:], self.v["alog"][:],
                                                             ACT.Exp, bias=0.0, scale=1.0),
                            sig=True)
        em.wait("vector", self.A_tok0)
        self.A_tok = em.op("vector", lambda E: E.tensor_scalar_mul(self.Aneg[:], self.Aneg[:],
                                                                   -1.0), sig=True)

        self.carry0 = self.sb("carry0", DC)
        self.carry1 = self.sb("carry1", DC * NST)
        self.halo1 = self.sb("halo1", DC * 3)
        self.st0 = self.sb("st0", DC)
        self.st1 = self.sb("st1", DC * NST)
        self.hal = self.sb("hal", DC * 3)

        self.NSLOTB = 80
        self.ringb = self.stk.enter_context(
            nc.sbuf_tensor("wringb", [128, self.NSLOTB * 128], BF16))
        self.ringb_free = [None] * self.NSLOTB
        self.ringb_head = 0

        self.bcB = self.sb("bcB", TC)
        self.bcC = self.sb("bcC", TC)
        self.bcS = self.sb("bcS", TC)

        self.ssm_slot = 0

        W512 = 512
        self.c_a = self.sb("c_a", DC)
        setup_scope = ExitStack()
        ssb = lambda nm: setup_scope.enter_context(
            self.nc.sbuf_tensor(nm, [128, W512], F32))
        self.gpad = ssb("gpad")
        self.gpadp = ssb("gpadp")
        self.gate = ssb("gate")
        self.capad = ssb("capad")
        self.cat1 = ssb("cat1")
        self.cat2 = ssb("cat2")
        big_bc = bass.AP(self.d["bigrow"].tensor, 0, [[0, 128], [1, W512]])
        gtok = em.dma("sync", self.gpad[:], big_bc)
        em.wait("vector", gtok, self.init_tok)
        em.op("vector", lambda E: E.tensor_scalar_add(self.gpadp[:], self.gpad[:], 1.0))
        self.gate_tok = em.op("vector", lambda E: E.tensor_sub(self.gate[:], self.gpadp[:],
                                                               self.gpad[:]), sig=True)

        def round32(v):
            em.op("vector", lambda E: E.tensor_tensor(self.cat1[:], v[:], self.gpadp[:],
                                                      ALU.mult))
            em.op("vector", lambda E: E.tensor_tensor(self.cat2[:], self.cat1[:], v[:],
                                                      ALU.subtract))
            em.op("vector", lambda E: E.tensor_tensor(self.cat1[:], self.cat1[:],
                                                      self.cat2[:], ALU.subtract))
            em.op("vector", lambda E: E.tensor_tensor(self.cat2[:], self.cat1[:], v[:],
                                                      ALU.subtract))
            return em.op("vector", lambda E: E.scalar_tensor_tensor(
                v[:], self.cat2[:], self.gate[:, 0:1], v[:], ALU.mult, ALU.add), sig=True)

        zt = em.op("vector", lambda E: E.memset(self.capad[:], 0.0), sig=True)
        em.wait("sync", zt)
        la_tok = em.dma("sync", self.capad[:, 0:DC], self.d["log_a_t"])
        em.wait("scalar", la_tok)
        e_tok = em.op("scalar", lambda E: E.activation(self.capad[:], self.capad[:],
                                                       ACT.Exp, bias=0.0, scale=-1.0),
                      sig=True)
        em.wait("vector", e_tok, self.gate_tok)
        round32(self.capad)
        em.op("vector", lambda E: E.tensor_scalar_add(self.capad[:], self.capad[:], 1.0))
        round32(self.capad)
        em.op("vector", lambda E: E.reciprocal(self.capad[:], self.capad[:]))
        ab_tok = round32(self.capad)
        em.wait("scalar", ab_tok)
        em.op("scalar", lambda E: E.activation(self.capad[:], self.capad[:],
                                               ACT.Ln, bias=0.0, scale=1.0))
        ca8_tok = em.op("scalar", lambda E: E.mul(self.capad[:], self.capad[:], 8.0),
                        sig=True)
        em.wait("vector", ca8_tok)
        self.ca_tok = em.op("vector", lambda E: E.tensor_copy(self.c_a[:], self.capad[:, 0:DC]),
                            sig=True)
        self.setup_scope = setup_scope


        self.prep_weights()
        self.chunk_guard = [self.prep_done]

        em.wait("sync", self.ca_tok)
        self.setup_scope.close()
        if not SKIP_P1:
            self.phase(1)
        else:
            em.op("vector", lambda E: E.memset(self.carry0[:], 0.0))
            em.op("vector", lambda E: E.memset(self.carry1[:], 0.0))
            self.ph1_tok = em.op("vector", lambda E: E.memset(self.halo1[:], 0.0), sig=True)
            self.scan0_tok = self.ph1_tok
            self.scan1_tok = self.ph1_tok
        self.phase(2)

        em.wait("gpsimd", self.out_toks[-1])
        em.op("gpsimd", lambda E: E.nop())

        with nc.Block() as block:
            em.finalize(block)
        self.stk.close()
        return nc


    def prep_weights(self):
        em = self.em
        nc = self.nc
        self.wready = {}
        self._wsync_done = set()
        self._ring_lane = 0
        order = ["wr", "wi", "wo0", "w1_0", "w2_0", "wip", "wop", "w1_1", "w2_1"]
        with ExitStack() as ps:
            buf = [ps.enter_context(nc.sbuf_tensor(f"prep{i}", [128, HC * 128], F32))
                   for i in range(2)]
            bhi = [ps.enter_context(nc.sbuf_tensor(f"prephi{i}", [128, HC * 128], BF16))
                   for i in range(2)]
            blo = [ps.enter_context(nc.sbuf_tensor(f"preplo{i}", [128, HC * 128], BF16))
                   for i in range(2)]
            free_tok = [None, None]
            i = 0
            for wnm in order:
                hi_t, lo_t, mc_n, kc_n = self.wsplit[wnm]
                cols = kc_n * 128
                toks = []
                for mc in range(mc_n):
                    b = i % 2
                    i += 1
                    em.wait("sync", free_tok[b])
                    ld = em.dma("sync", buf[b][:, 0:cols], self.d[wnm][mc],
                                serial=True)
                    em.wait("scalar", ld)
                    h_tok = em.op("scalar", lambda E, o=bhi[b][:, 0:cols],
                                  s=buf[b][:, 0:cols]:
                                  E.activation(o, s, ACT.Identity, bias=0.0, scale=1.0),
                                  sig=True)
                    em.wait("vector", h_tok)
                    l_tok = em.op("vector", lambda E, o=blo[b][:, 0:cols],
                                  s=buf[b][:, 0:cols], h2=bhi[b][:, 0:cols]:
                                  E.tensor_tensor(o, s, h2, ALU.subtract), sig=True)
                    em.wait("sync", l_tok)
                    t1 = em.dma("sync", hi_t[mc], bhi[b][:, 0:cols], serial=True)
                    t2 = em.dma("sync", lo_t[mc], blo[b][:, 0:cols], serial=True)
                    free_tok[b] = t2
                    toks.append(t2)
                self.wready[wnm] = toks[-1]
            self.prep_done = toks[-1]

    def wload(self, hbm_block_ap, n_tiles):
        runs = []
        need = n_tiles
        while need > 0:
            take = min(need, 16)
            if self.ring_head + take > self.NSLOT:
                self.ring_head = 0
            s0 = self.ring_head
            self.ring_head += take
            runs.append((s0, take))
            need -= take
        slots = []
        toks = []
        col0 = 0
        for s0, take in runs:
            for s in range(s0, s0 + take):
                self.em.wait("sync", self.ring_free[s])
            t = self.em.dma("sync", self.ring[:, s0 * 128:(s0 + take) * 128],
                            hbm_block_ap[:, col0 * 128:(col0 + take) * 128])
            toks.append(t)
            slots.extend(range(s0, s0 + take))
            col0 += take
        return slots, toks

    def mm_block(self, w_hbm_ap, n_k, rhs_tiles, rhs_tok, bank, extra=()):
        em = self.em
        slots, dtoks = self.wload(w_hbm_ap, n_k)
        pb = self.pb[bank]
        em.wait("tensor", rhs_tok, self.pb_free[bank], *dtoks, *extra)
        mm_tok = None
        for j, kc in enumerate(range(n_k)):
            s = slots[j]
            w_ap = self.ring[:, s * 128:(s + 1) * 128]
            mm_tok = em.op("tensor", lambda E, pb=pb, w=w_ap, r=rhs_tiles[kc],
                           a=(kc == 0), o=(kc == n_k - 1):
                           E.matmul(pb[:], w, r[:, 0:TC], start=a, stop=o),
                           sig=(kc == n_k - 1))
        for s in slots:
            self.ring_free[s] = mm_tok
        return mm_tok


    def wloadb(self, hbm_block_ap, n_tiles, lane=None):
        runs = []
        need = n_tiles
        while need > 0:
            take = min(need, 16)
            if self.ringb_head + take > self.NSLOTB:
                self.ringb_head = 0
            s0 = self.ringb_head
            self.ringb_head += take
            runs.append((s0, take))
            need -= take
        slots = []
        toks = []
        col0 = 0
        for s0, take in runs:
            for s in range(s0, s0 + take):
                self.em.wait("sync", self.ringb_free[s])
            t = self.em.dma("sync", self.ringb[:, s0 * 128:(s0 + take) * 128],
                            hbm_block_ap[:, col0 * 128:(col0 + take) * 128],
                            sem_name=lane)
            toks.append(t)
            slots.extend(range(s0, s0 + take))
            col0 += take
        return slots, toks

    def mm_block3(self, wnm, mc, n_k, rhs_hi, rhs_lo, rhs_tok, bank, extra=()):
        em = self.em
        hi_t, lo_t, _, _ = self.wsplit[wnm]
        pb = self.pb[bank]
        if wnm not in self._wsync_done:
            em.wait("sync", self.wready.get(wnm))
            self._wsync_done.add(wnm)
        rt_list = rhs_tok if isinstance(rhs_tok, list) else None
        em.wait("tensor", None if rt_list else rhs_tok, self.pb_free[bank],
                self.wready.get(wnm), *extra)
        mm_tok = None
        n_terms = 3 * n_k
        j = 0
        for k0 in range(0, n_k, 8):
            kn = min(8, n_k - k0)
            lane = f"rg{self._ring_lane}"
            self._ring_lane = (self._ring_lane + 1) % 8
            s_hi, d1 = self.wloadb(hi_t[mc][:, k0 * 128:(k0 + kn) * 128], kn,
                                   lane=lane)
            s_lo, d2 = self.wloadb(lo_t[mc][:, k0 * 128:(k0 + kn) * 128], kn,
                                   lane=lane)
            rt = ([rt_list[min(k0 + kn - 1, len(rt_list) - 1)]]
                  if rt_list else [])
            em.wait("tensor", *d1, *d2, *rt)
            for kc in range(kn):
                wh = self.ringb[:, s_hi[kc] * 128:(s_hi[kc] + 1) * 128]
                wl = self.ringb[:, s_lo[kc] * 128:(s_lo[kc] + 1) * 128]
                last_sub = (kc == kn - 1)
                for ti, (w_ap, r_ap) in enumerate(((wh, rhs_hi[k0 + kc]),
                                                   (wh, rhs_lo[k0 + kc]),
                                                   (wl, rhs_hi[k0 + kc]))):
                    mm_tok = em.op("tensor", lambda E, pb=pb, w=w_ap, r=r_ap,
                                   a=(j == 0), o=(j == n_terms - 1):
                                   E.matmul(pb[:], w, r[:, 0:TC], start=a, stop=o),
                                   sig=(last_sub and ti == 2))
                    j += 1
            for s in s_hi:
                self.ringb_free[s] = mm_tok
            for s in s_lo:
                self.ringb_free[s] = mm_tok
        return mm_tok

    def split(self, tset_fn, nm, tiles, tok, n=DC):
        em = self.em
        th = tset_fn(f"{nm}_hi", TC, BF16)
        tl = tset_fn(f"{nm}_lo", TC, BF16)
        toks = []
        for dc in range(n):
            em.wait("scalar", tok)
            hcast = em.op("scalar", lambda E, o=th[dc], s=tiles[dc]:
                          E.activation(o[:], s[:, 0:TC], ACT.Identity,
                                       bias=0.0, scale=1.0), sig=True)
            em.wait("vector", hcast)
            toks.append(em.op("vector",
                              lambda E, o=tl[dc], s=tiles[dc], h2=th[dc]:
                              E.tensor_tensor(o[:], s[:, 0:TC], h2[:],
                                              ALU.subtract), sig=True))
        return th, tl, toks

    def keeper(self, tok):
        em = self.em
        em.wait("tensor", tok)
        em.op("tensor", lambda E: E.matmul(self.pb[4][64:65, 0:1],
                                           self.ones_row[0:1, 0:1],
                                           self.ones_col[0:1, 0:1],
                                           start=True, stop=True))

    def bcast_row_psum(self, row_ap, bank, tok):
        em = self.em
        pb = self.pb[bank]
        em.wait("tensor", tok, self.pb_free[bank])
        return em.op("tensor", lambda E: E.matmul(pb[:], self.ones_row[:], row_ap,
                                                  start=True, stop=True), sig=True)

    def ln_stats(self, y, y_tok, sq, bank_s, bank_q, eps_ap, want_mean):
        em = self.em
        pbs, pbq = self.pb[bank_s], self.pb[bank_q]
        sq_tok = None
        for dc in range(DC):
            em.wait("scalar", y_tok)
            sq_tok = em.op("scalar", lambda E, o=sq[dc], i=y[dc]:
                           E.activation(o[:, 0:TC], i[:, 0:TC], ACT.Square,
                                        bias=0.0, scale=1.0), sig=(dc == DC - 1))
        em.wait("tensor", y_tok, sq_tok, self.pb_free[bank_s], self.pb_free[bank_q])
        if want_mean:
            for dc in range(DC):
                em.op("tensor", lambda E, r=y[dc], a=(dc == 0), o=(dc == DC - 1):
                      E.matmul(pbs[0:1, :], self.ones_col[:], r[:, 0:TC],
                               start=a, stop=o))
        st_tok = None
        for dc in range(DC):
            st_tok = em.op("tensor", lambda E, r=sq[dc], a=(dc == 0), o=(dc == DC - 1):
                           E.matmul(pbq[0:1, :], self.ones_col[:], r[:, 0:TC],
                                    start=a, stop=o), sig=(dc == DC - 1))
        em.wait("vector", st_tok)
        if want_mean:
            em.op("vector", lambda E: E.tensor_scalar_mul(self.row_m[:], pbs[0:1, :], 1.0 / D))
            em.op("vector", lambda E: E.tensor_scalar_mul(self.row_r[:], pbq[0:1, :], 1.0 / D))
            em.op("vector", lambda E: E.tensor_tensor(self.row_t[:], self.row_m[:],
                                                      self.row_m[:], ALU.mult))
            v_tok = em.op("vector", lambda E: E.tensor_sub(self.row_r[:], self.row_r[:],
                                                           self.row_t[:]), sig=True)
        else:
            v_tok = em.op("vector", lambda E: E.tensor_scalar_mul(self.row_r[:], pbq[0:1, :],
                                                                  1.0 / D), sig=True)
        em.wait("scalar", v_tok)
        s_tok = em.op("scalar", lambda E: E.activation(self.row_r[:], self.row_r[:],
                                                       ACT.Sqrt, bias=eps_ap, scale=1.0),
                      sig=True)
        em.wait("vector", s_tok)
        if want_mean:
            em.op("vector", lambda E: E.reciprocal(self.row_r[:], self.row_r[:]))
            out_tok = em.op("vector", lambda E: E.tensor_tensor(self.row_t[:], self.row_m[:],
                                                                self.row_r[:], ALU.mult),
                            sig=True)
        else:
            out_tok = em.op("vector", lambda E: E.reciprocal(self.row_r[:], self.row_r[:]),
                            sig=True)
        self.pb_free[bank_s] = out_tok
        self.pb_free[bank_q] = out_tok
        return out_tok

    def layernorm(self, y, y_tok, out, sq):
        em = self.em
        st = self.ln_stats(y, y_tok, sq, 4, 5, self.eps_ln[0:1, 0:1], True)
        b_r = self.bcast_row_psum(self.row_r[:], 6, st)
        b_mr = self.bcast_row_psum(self.row_t[:], 7, st)
        pbr, pbm = self.pb[6], self.pb[7]
        o_tok = None
        for dc in range(DC):
            em.wait("vector", b_r, b_mr)
            em.op("vector", lambda E, o=out[dc], i=y[dc]:
                  E.tensor_tensor(o[:, 0:TC], i[:, 0:TC], pbr[:], ALU.mult))
            o_tok = em.op("vector", lambda E, o=out[dc]:
                          E.tensor_tensor(o[:, 0:TC], o[:, 0:TC], pbm[:], ALU.subtract),
                          sig=(dc == DC - 1))
        self.pb_free[6] = o_tok
        self.pb_free[7] = o_tok
        return o_tok

    def phase(self, phase):
        em = self.em
        is_p1 = (phase == 1)
        xin_hbm = self.d["x_pre"] if is_p1 else self.d["x_own"]

        if is_p1:
            em.wait("vector", self.init_tok)
            em.op("vector", lambda E: E.memset(self.st0[:], 0.0))
            em.op("vector", lambda E: E.memset(self.st1[:], 0.0))
            self.state_tok = em.op("vector", lambda E: E.memset(self.hal[:], 0.0), sig=True)
        else:
            em.wait("vector", self.ph1_tok, self.vtok["flag"])
            em.op("vector", lambda E: E.tensor_scalar(self.st0[:], self.carry0[:],
                                                      self.flag[:, 0:1], None, ALU.mult))
            em.op("vector", lambda E: E.tensor_scalar(self.st1[:], self.carry1[:],
                                                      self.flag[:, 0:1], None, ALU.mult))
            self.state_tok = em.op("vector", lambda E:
                                   E.tensor_scalar(self.hal[:], self.halo1[:],
                                                   self.flag[:, 0:1], None, ALU.mult),
                                   sig=True)

        for tc in range(LIMIT_NTC or NTC):
            self.chunk(phase, tc, xin_hbm)

        if is_p1:
            if not hasattr(self, 'scan1_tok'):
                self.scan1_tok = self.scan0_tok if hasattr(self, 'scan0_tok') else None
            if not hasattr(self, 'scan0_tok'):
                self.scan0_tok = None
            em.wait("vector", self.scan0_tok, self.scan1_tok)
            em.op("vector", lambda E: E.tensor_copy(self.carry0[:], self.st0[:]))
            em.op("vector", lambda E: E.tensor_copy(self.carry1[:], self.st1[:]))
            self.ph1_tok = em.op("vector", lambda E: E.tensor_copy(self.halo1[:], self.hal[:]),
                                 sig=True)

    def chunk(self, phase, tc, xin_hbm):
        em = self.em
        nc = self.nc
        is_p1 = (phase == 1)
        nm = f"p{phase}t{tc}"

        with ExitStack() as sc:
            def tset(name, cols=TC, dtype=F32):
                t = sc.enter_context(nc.sbuf_tensor(f"{nm}_{name}", [128, DC * cols], dtype))
                return [t[:, i * cols:(i + 1) * cols] for i in range(DC)]

            em.wait("sync", *self.chunk_guard)
            em.wait("gpsimd", *self.chunk_guard)
            rs = tset("rs")

            self.l0_attn(nm, sc, tset, xin_hbm, rs, phase, tc)
            if STAGE_LIMIT in (None, 'ffn', 'l1'):
                self.ffn(nm + "f0", rs, self.rs_tok, "w1_0", "b1_0", "w2_0", "b2_0", phase, tc)
            if STAGE_LIMIT in (None, 'l1'):
                self.l1_attn(nm, sc, tset, rs, self.rs_tok, phase, tc)
            if not is_p1 and STAGE_LIMIT is None:
                self.ffn(nm + "f1", rs, self.rs_tok, "w1_1", "b1_1", "w2_1", "b2_1", phase, tc)
                self.tap("out", rs, self.rs_tok, phase, tc)
                for dc in range(DC):
                    em.wait("gpsimd", self.rs_tok)
                    t = em.dma("gpsimd", self.d["out"][dc, :, tc * TC:(tc + 1) * TC],
                               rs[dc][:, 0:TC])
                    self.out_toks.append(t)
            elif not is_p1:
                tiles = getattr(self, 'dump_tiles', None) or rs
                for dc in range(DC):
                    em.wait("gpsimd", self.rs_tok)
                    t = em.dma("gpsimd", self.d["out"][dc, :, tc * TC:(tc + 1) * TC],
                               tiles[dc][:, 0:TC])
                    self.out_toks.append(t)
            ks = getattr(self, 'keep_scope', None)
            if ks is not None:
                ks.close()
                self.keep_scope = None
                self.dump_tiles = None
            self.chunk_guard = [self.em.tokq("vector"), self.em.tokq("scalar"),
                                self.em.tokq("tensor"), self.em.tokq("gpsimd")]

    def l0_attn(self, nm, sc_outer, tset_outer, xin_hbm, rs, phase, tc):
        em = self.em
        nc = self.nc
        v = self.v
        lsc = ExitStack()
        def tset(name, cols=TC, dtype=F32):
            t = lsc.enter_context(nc.sbuf_tensor(f"{nm}0_{name}", [128, DC * cols], dtype))
            return [t[:, i * cols:(i + 1) * cols] for i in range(DC)]
        x = tset("x", TC + 3)
        x_tok = None
        for dc in range(DC):
            x_tok = em.dma("sync", x[dc], xin_hbm[dc, :, tc * TC: tc * TC + TC + 3])
        xc = tset("xc")
        c_tok = None
        for dc in range(DC):
            em.wait("vector", x_tok, self.vtok["cb0"])
            em.op("vector", lambda E, o=xc[dc], i=x[dc], dc=dc:
                  E.tensor_scalar(o[:], i[:, 3:3 + TC],
                                  v["cw0"][:, dc * KER + KER - 1:dc * KER + KER],
                                  v["cb0"][:, dc:dc + 1], ALU.mult, ALU.add))
            for k in range(KER - 1):
                c_tok = em.op("vector", lambda E, o=xc[dc], i=x[dc], dc=dc, k=k:
                              E.scalar_tensor_tensor(o[:], i[:, k:k + TC],
                                                     v["cw0"][:, dc * KER + k:dc * KER + k + 1],
                                                     o[:], ALU.mult, ALU.add),
                              sig=(dc == DC - 1 and k == KER - 2))
        self.tap("xc0", xc, c_tok, phase, tc)
        if STAGE_LIMIT == 'conv':
            self.rs_tok = c_tok
            self.dump_tiles = xc
            self.scan0_tok = c_tok
            self.keep_scope = lsc
            return
        xch, xcl, xcsp_tok = self.split(tset, "xcs0", xc, c_tok)
        r = tset("r")
        ii = tset("i")
        for mc in range(DC):
            bank = mc % 2
            mm = self.mm_block3("wr", mc, DC, xch, xcl, xcsp_tok, bank)
            em.wait("scalar", mm, self.vtok["wr_b"])
            e1 = em.op("scalar", lambda E, o=r[mc], pb=self.pb[bank], mc=mc:
                       E.activation(o[:], pb[:], ACT.Sigmoid,
                                    bias=v["wr_b"][:, mc:mc + 1], scale=1.0), sig=True)
            self.pb_free[bank] = e1
            bank = 2 + mc % 2
            mm = self.mm_block3("wi", mc, DC, xch, xcl, xcsp_tok, bank)
            em.wait("scalar", mm, self.vtok["wi_b"])
            e2 = em.op("scalar", lambda E, o=ii[mc], pb=self.pb[bank], mc=mc:
                       E.activation(o[:], pb[:], ACT.Sigmoid,
                                    bias=v["wi_b"][:, mc:mc + 1], scale=1.0), sig=True)
            self.pb_free[bank] = e2
        ri_tok = e2
        rg_t = self.tap("rg", r, ri_tok, phase, tc)
        ig_t = self.tap("ig", ii, ri_tok, phase, tc)
        if STAGE_LIMIT == 'gates':
            self.rs_tok = ri_tok
            self.dump_tiles = r
            self.scan0_tok = ri_tok
            self.keep_scope = lsc
            return
        a = r
        a_tok = None
        for dc in range(DC):
            em.wait("scalar", self.ca_tok, rg_t)
            a_tok = em.op("scalar", lambda E, o=a[dc], dc=dc:
                          E.activation(o[:], o[:], ACT.Exp, bias=0.0,
                                       scale=self.c_a[:, dc:dc + 1]), sig=(dc == DC - 1))
        b = tset("b")
        b_tok = None
        for dc in range(DC):
            em.wait("vector", a_tok)
            em.op("vector", lambda E, t=b[dc], i=a[dc]:
                  E.tensor_tensor(t[:], i[:], i[:], ALU.mult))
            em.op("vector", lambda E, t=b[dc]:
                  E.tensor_scalar(t[:], t[:], -1.0, 1.0, ALU.mult, ALU.add))
            tv = em.op("vector", lambda E, t=b[dc]:
                       E.tensor_scalar_max(t[:], t[:], 1e-6), sig=True)
            em.wait("scalar", tv)
            s_tok = em.op("scalar", lambda E, t=b[dc]:
                          E.activation(t[:], t[:], ACT.Sqrt, bias=0.0, scale=1.0), sig=True)
            em.wait("vector", s_tok)
            em.op("vector", lambda E, t=b[dc], i=ii[dc]:
                  E.tensor_tensor(t[:], t[:], i[:], ALU.mult))
            b_tok = em.op("vector", lambda E, t=b[dc], c=xc[dc]:
                          E.tensor_tensor(t[:], t[:], c[:], ALU.mult),
                          sig=(dc == DC - 1))
        aa_t = self.tap("aa", a, a_tok, phase, tc)
        bb_t = self.tap("bb", b, b_tok, phase, tc)
        h = ii
        sc_tok = None
        em.wait("vector", b_tok, self.state_tok, ig_t, aa_t, bb_t)
        for dc in range(DC):
            t = em.op("vector", lambda E, o=h[dc], A=a[dc], Bb=b[dc], dc=dc:
                      E.tensor_tensor_scan(o[:], A[:], Bb[:], self.st0[:, dc:dc + 1],
                                           ALU.mult, ALU.add), sig=True)
            self.keeper(t)
        for dc in range(DC):
            sc_tok = em.op("vector", lambda E, o=h[dc], dc=dc:
                           E.tensor_copy(self.st0[:, dc:dc + 1], o[:, TC - 1:TC]),
                           sig=(dc == DC - 1))
        self.scan0_tok = sc_tok
        self.tap("h0", h, sc_tok, phase, tc)
        if STAGE_LIMIT == 'scan':
            self.rs_tok = sc_tok
            self.dump_tiles = h
            self.keep_scope = lsc
            return
        hsp_h, hsp_l, hsp_tok = self.split(tset, "hsp", h, sc_tok)
        o = a
        sq = b
        ev_tok = None
        for mc in range(DC):
            bank = mc % 2
            mm = self.mm_block3("wo0", mc, DC, hsp_h, hsp_l, hsp_tok, bank)
            em.wait("scalar", mm)
            em.op("scalar", lambda E, o2=o[mc], pb=self.pb[bank]:
                  E.activation(o2[:], pb[:], ACT.Identity, bias=0.0, scale=1.0))
            ev_tok = em.op("scalar", lambda E, s=sq[mc], pb=self.pb[bank]:
                           E.activation(s[:], pb[:], ACT.Square, bias=0.0, scale=1.0),
                           sig=True)
            self.pb_free[bank] = ev_tok
        st = self.ln_stats(o, ev_tok, sq, 4, 5, self.eps_rms[0:1, 0:1], False)
        bc = self.bcast_row_psum(self.row_r[:], 6, st)
        y1 = h
        y_tok = None
        for dc in range(DC):
            em.wait("vector", bc, self.vtok["norm_w"])
            em.op("vector", lambda E, t=o[dc]:
                  E.tensor_tensor(t[:], t[:], self.pb[6][:], ALU.mult))
            y_tok = em.op("vector", lambda E, y=y1[dc], t=o[dc], xx=x[dc], dc=dc:
                          E.scalar_tensor_tensor(y[:], t[:], v["norm_w"][:, dc:dc + 1],
                                                 xx[:, 3:3 + TC], ALU.mult, ALU.add),
                          sig=(dc == DC - 1))
        self.pb_free[6] = y_tok
        self.tap("y1", y1, y_tok, phase, tc)
        self.rs_tok = self.layernorm(y1, y_tok, rs, sq)
        self.tap("ln1", rs, self.rs_tok, phase, tc)
        lsc.close()

    def ffn(self, nm, rs, rs_tok, w1, b1, w2, b2, phase, tc):
        em = self.em
        nc = self.nc
        v = self.v
        with ExitStack() as fsc:
            def ftset(name, cols=TC, dtype=F32, n=DC):
                t = fsc.enter_context(nc.sbuf_tensor(f"{nm}_{name}", [128, n * cols], dtype))
                return [t[:, i * cols:(i + 1) * cols] for i in range(n)]
            rsh, rsl, rsp_tok = self.split(ftset, "rsp", rs, rs_tok)
            hid_h = ftset("hidh", TC, BF16, HC)
            hid_l = ftset("hidl", TC, BF16, HC)
            y2 = ftset("y2")
            sqs = ftset("sq")
            hft = ftset("hft", TC, F32, 2)
            hft_guard = [None, None]
            h_tok = None
            for hc in range(HC):
                bank = hc % 2
                mm = self.mm_block3(w1, hc, DC, rsh, rsl, rsp_tok, bank)
                em.wait("scalar", mm, self.vtok[b1], hft_guard[hc % 2])
                hf_tok = em.op("scalar", lambda E, o=hft[hc % 2], pb=self.pb[bank], hc=hc:
                               E.activation(o[:], pb[:], ACT.Relu,
                                            bias=v[b1][:, hc:hc + 1], scale=1.0), sig=True)
                self.pb_free[bank] = hf_tok
                hh_tok = em.op("scalar", lambda E, o=hid_h[hc], s=hft[hc % 2]:
                               E.activation(o[:], s[:], ACT.Identity,
                                            bias=0.0, scale=1.0), sig=True)
                em.wait("vector", hh_tok)
                h_tok = em.op("vector", lambda E, o=hid_l[hc], s=hft[hc % 2], h2=hid_h[hc]:
                              E.tensor_tensor(o[:], s[:], h2[:], ALU.subtract),
                              sig=True)
                hft_guard[hc % 2] = h_tok
            e_tok = None
            for oc in range(DC):
                bank = 2 + oc % 2
                mm = self.mm_block3(w2, oc, HC, hid_h, hid_l, h_tok, bank)
                em.wait("scalar", mm, self.vtok[b2])
                ev = em.op("scalar", lambda E, o=y2[oc], pb=self.pb[bank], oc=oc:
                           E.activation(o[:], pb[:], ACT.Identity,
                                        bias=v[b2][:, oc:oc + 1], scale=1.0), sig=True)
                self.pb_free[bank] = ev
                em.wait("vector", ev)
                e_tok = em.op("vector", lambda E, o=y2[oc], r=rs[oc]:
                              E.tensor_add(o[:], o[:], r[:]), sig=(oc == DC - 1))
            self.rs_tok = self.layernorm(y2, e_tok, rs, sqs)
            self.tap("ffn" + nm[-2:], rs, self.rs_tok, phase, tc)

    def l1_attn(self, nm, sc_outer, tset_outer, rs, rs_tok, phase, tc):
        em = self.em
        nc = self.nc
        v = self.v
        is_p1 = (phase == 1)
        with ExitStack() as lsc:
            def tset(name, cols=TC, dtype=F32):
                t = lsc.enter_context(nc.sbuf_tensor(f"{nm}L_{name}", [128, DC * cols], dtype))
                return [t[:, i * cols:(i + 1) * cols] for i in range(DC)]

            xin = tset("xin", TC + 3)
            z = tset("z") if not is_p1 else None
            with ExitStack() as rsc:
                def rtset(name, cols=TC, dtype=F32):
                    t = rsc.enter_context(
                        nc.sbuf_tensor(f"{nm}Lr_{name}", [128, DC * cols], dtype))
                    return [t[:, i * cols:(i + 1) * cols] for i in range(DC)]
                rsh, rsl, rsp_tok = self.split(rtset, "rsp1", rs, rs_tok)
                ip_tok = None
                for mc in range(DC):
                    bank = mc % 2
                    mm = self.mm_block3("wip", mc, DC, rsh, rsl, rsp_tok, bank)
                    em.wait("scalar", mm)
                    ip_tok = em.op("scalar", lambda E, o=xin[mc], pb=self.pb[bank]:
                                   E.activation(o[:, 3:3 + TC], pb[:], ACT.Identity,
                                                bias=0.0, scale=1.0), sig=True)
                    self.pb_free[bank] = ip_tok
                if not is_p1:
                    for mc in range(DC):
                        bank = 2 + mc % 2
                        mm = self.mm_block3("wip", DC + mc, DC, rsh, rsl, rsp_tok, bank)
                        em.wait("scalar", mm)
                        zt = em.op("scalar", lambda E, o=z[mc], pb=self.pb[bank]:
                                   E.activation(o[:], pb[:], ACT.Identity,
                                                bias=0.0, scale=1.0), sig=True)
                        self.pb_free[bank] = zt
                self.rsp1_guard = ip_tok if is_p1 else zt
            hl_tok = None
            for dc in range(DC):
                em.wait("gpsimd", ip_tok, self.state_tok if tc == 0 else self.halo_tok)
                hl_tok = em.op("gpsimd", lambda E, o=xin[dc], dc=dc:
                               E.tensor_copy(o[:, 0:3], self.hal[:, dc * 3:dc * 3 + 3]),
                               sig=(dc == DC - 1))
            nh_tok = None
            for dc in range(DC):
                em.wait("vector", ip_tok, hl_tok)
                nh_tok = em.op("vector", lambda E, i=xin[dc], dc=dc:
                               E.tensor_copy(self.hal[:, dc * 3:dc * 3 + 3],
                                             i[:, TC:TC + 3]),
                               sig=(dc == DC - 1))
            self.halo_tok = nh_tok
            xcs = tset("xcs")
            cs_tok = None
            for dc in range(DC):
                em.wait("vector", hl_tok, self.vtok["cb1"], self.rsp1_guard)
                em.op("vector", lambda E, o=xcs[dc], i=xin[dc], dc=dc:
                      E.tensor_scalar(o[:], i[:, 3:3 + TC],
                                      v["cw1"][:, dc * KER + KER - 1:dc * KER + KER],
                                      v["cb1"][:, dc:dc + 1], ALU.mult, ALU.add))
                for k in range(KER - 1):
                    cs_tok = em.op("vector", lambda E, o=xcs[dc], i=xin[dc], dc=dc, k=k:
                                   E.scalar_tensor_tensor(o[:], i[:, k:k + TC],
                                                          v["cw1"][:, dc * KER + k:dc * KER + k + 1],
                                                          o[:], ALU.mult, ALU.add),
                                   sig=(dc == DC - 1 and k == KER - 2))
            sl_tok = None
            for dc in range(DC):
                em.wait("scalar", cs_tok)
                sl_tok = em.op("scalar", lambda E, o=xcs[dc]:
                               E.activation(o[:], o[:], ACT.Silu, bias=0.0, scale=1.0),
                               sig=(dc == DC - 1))
            self.tap("xcs", xcs, sl_tok, phase, tc)
            pb4 = self.pb[4]
            em.wait("tensor", sl_tok, self.pb_free[4], self.vtok["wxp"])
            xp_tok = None
            for kc in range(DC):
                xp_tok = em.op("tensor", lambda E, kc=kc:
                               E.matmul(pb4[0:33, :], self.wxp[:, kc * 33:(kc + 1) * 33],
                                        xcs[kc][:, 0:TC], start=(kc == 0),
                                        stop=(kc == DC - 1)), sig=(kc == DC - 1))
            self.xp_done = xp_tok
            ssm = lsc.enter_context(nc.sbuf_tensor(f"{nm}L_ssm", [33, TC], F32))
            em.wait("vector", xp_tok)
            ssm_tok = em.op("vector", lambda E: E.tensor_copy(ssm[:], pb4[0:33, :]), sig=True)
            self.pb_free[4] = ssm_tok
            slot = self.ssm_slot
            self.ssm_slot ^= 1
            em.wait("sync", ssm_tok, getattr(self, f"_ssm_guard{slot}", None))
            ssm_hbm_tok = em.dma("sync", self.d["ssm_hbm"][slot], ssm[:])
            em.wait("sync", ssm_hbm_tok)
            row = self.d["ssm_hbm"].tensor
            def hrow(r):
                off = (slot * 33 + r) * TC
                return bass.AP(row, off, [[0, 128], [1, TC]])
            s_tok = em.dma("sync", self.bcS[:], hrow(32))
            dt = tset("dt")
            dts_t = lsc.enter_context(nc.sbuf_tensor(f"{nm}L_dts", [128, 2 * TC], F32))
            dts = [dts_t[:, 0:TC], dts_t[:, TC:2 * TC]]
            dts_guard = [None, None]
            dtx = xcs
            d_tok = None
            for dc in range(DC):
                em.wait("scalar", s_tok, self.vtok["dtb"])
                em.op("scalar", lambda E, o=dt[dc], dc=dc:
                      E.activation(o[:], self.bcS[:], ACT.Abs,
                                   bias=v["dtb"][:, dc:dc + 1], scale=v["dtw"][:, dc:dc + 1]))
                em.op("scalar", lambda E, o=dt[dc]:
                      E.activation(o[:], o[:], ACT.Exp, bias=0.0, scale=-1.0))
                em.op("scalar", lambda E, o=dt[dc]:
                      E.activation(o[:], o[:], ACT.Ln,
                                   bias=self.ones_col[:, 0:1], scale=1.0))
                em.wait("scalar", dts_guard[dc % 2])
                s2 = em.op("scalar", lambda E, o=dts[dc % 2], dc=dc:
                           E.activation(o[:], self.bcS[:], ACT.Relu,
                                        bias=v["dtb"][:, dc:dc + 1],
                                        scale=v["dtw"][:, dc:dc + 1]), sig=True)
                em.wait("vector", s2)
                d_tok = em.op("vector", lambda E, o=dt[dc], p=dts[dc % 2]:
                              E.tensor_add(o[:], o[:], p[:]), sig=True)
                dts_guard[dc % 2] = d_tok
            self.tap("dt", dt, d_tok, phase, tc)
            dx_tok = None
            for dc in range(DC):
                em.wait("vector", d_tok, sl_tok, self.xp_done)
                dx_tok = em.op("vector", lambda E, o=dtx[dc], a2=dt[dc]:
                               E.tensor_tensor(o[:], a2[:], o[:], ALU.mult),
                               sig=(dc == DC - 1))
            dA = tset("dA")
            dBx = tset("dBx")
            hh = tset("hh")
            y = tset("y") if not is_p1 else None
            ya_tok = None
            scn_tok = None
            prev_hc_tok = None
            prev_bx = None
            for n in range(NST):
                em.wait("sync", prev_bx, prev_hc_tok)
                b_tok = em.dma("sync", self.bcB[:], hrow(n))
                c_tok = em.dma("sync", self.bcC[:], hrow(NST + n)) if not is_p1 else None
                aw = None
                for dc in range(DC):
                    em.wait("scalar", d_tok, self.A_tok, scn_tok)
                    aw = em.op("scalar", lambda E, o=dA[dc], i=dt[dc], dc=dc, n=n:
                               E.activation(o[:], i[:], ACT.Exp, bias=0.0,
                                            scale=self.Aneg[:, dc * NST + n:dc * NST + n + 1]),
                               sig=(dc == DC - 1))
                bx = None
                for dc in range(DC):
                    em.wait("vector", b_tok, dx_tok, prev_hc_tok or scn_tok, ya_tok)
                    bx = em.op("vector", lambda E, o=dBx[dc], a2=dtx[dc]:
                               E.tensor_tensor(o[:], a2[:], self.bcB[:], ALU.mult),
                               sig=(dc == DC - 1))
                prev_bx = bx
                em.wait("vector", aw, bx, self.state_tok)
                for dc in range(DC):
                    col = dc * NST + n
                    t = em.op("vector", lambda E, o=hh[dc], A=dA[dc], Bb=dBx[dc], col=col:
                              E.tensor_tensor_scan(o[:], A[:], Bb[:],
                                                   self.st1[:, col:col + 1],
                                                   ALU.mult, ALU.add), sig=True)
                    self.keeper(t)
                for dc in range(DC):
                    col = dc * NST + n
                    scn_tok = em.op("vector", lambda E, o=hh[dc], col=col:
                                    E.tensor_copy(self.st1[:, col:col + 1], o[:, TC - 1:TC]),
                                    sig=(dc == DC - 1))
                if not is_p1:
                    hc_tok = None
                    for dc in range(DC):
                        em.wait("vector", scn_tok, c_tok, ya_tok)
                        hc_tok = em.op("vector", lambda E, o=dBx[dc], a2=hh[dc]:
                                       E.tensor_tensor(o[:], a2[:], self.bcC[:], ALU.mult),
                                       sig=(dc == DC - 1))
                    prev_hc_tok = hc_tok
                    for dc in range(DC):
                        em.wait("gpsimd", hc_tok)
                        if n == 0:
                            ya_tok = em.op("gpsimd", lambda E, o=y[dc], a2=dBx[dc]:
                                           E.tensor_copy(o[:], a2[:]),
                                           sig=(dc == DC - 1))
                        else:
                            ya_tok = em.op("gpsimd", lambda E, o=y[dc], a2=dBx[dc]:
                                           E.tensor_add(o[:], o[:], a2[:]),
                                           sig=(dc == DC - 1))
            self.scan1_tok = scn_tok
            setattr(self, f"_ssm_guard{slot}", scn_tok if is_p1 else ya_tok)
            if is_p1:
                self.rs_tok = rs_tok
                return
            self.tap("ymam", y, ya_tok, phase, tc)
            zs_tok = None
            for dc in range(DC):
                em.wait("scalar", zt)
                zs_tok = em.op("scalar", lambda E, o=z[dc]:
                               E.activation(o[:], o[:], ACT.Silu, bias=0.0, scale=1.0),
                               sig=(dc == DC - 1))
            ym_tok = None
            for dc in range(DC):
                em.wait("vector", zs_tok, ya_tok)
                ym_tok = em.op("vector", lambda E, o=y[dc], a2=z[dc]:
                               E.tensor_tensor(o[:], o[:], a2[:], ALU.mult),
                               sig=(dc == DC - 1))
            y3 = dt
            sqt = dA
            ysh, ysl, ysp_tok = self.split(tset, "ysp", y, ym_tok)
            e_tok = None
            for mc in range(DC):
                bank = mc % 2
                mm = self.mm_block3("wop", mc, DC, ysh, ysl, ysp_tok, bank)
                em.wait("scalar", mm)
                ev = em.op("scalar", lambda E, o=y3[mc], pb=self.pb[bank]:
                           E.activation(o[:], pb[:], ACT.Identity, bias=0.0, scale=1.0),
                           sig=True)
                self.pb_free[bank] = ev
                em.wait("vector", ev)
                e_tok = em.op("vector", lambda E, o=y3[mc], r2=rs[mc]:
                              E.tensor_add(o[:], o[:], r2[:]), sig=(mc == DC - 1))
            self.rs_tok = self.layernorm(y3, e_tok, rs, sqt)
            self.tap("ln3", rs, self.rs_tok, phase, tc)


def _tokq(self, eng, kind="c"):
    name = f"{eng}_{kind}"
    return (name, self.counts[name])


Em.tokq = _tokq



def _wblocks(wt):
    K, M = wt.shape
    return np.ascontiguousarray(
        wt.reshape(K // 128, 128, M // 128, 128).transpose(2, 1, 0, 3)
        .reshape(M // 128, 128, (K // 128) * 128))


def _feat(vec, cols=1):
    return np.ascontiguousarray(np.asarray(vec).reshape(DC, 128, cols))


def stage_inputs(inputs):
    x = np.asarray(inputs["L0_attn_x"])
    base = {}
    base["wr"] = _wblocks(np.ascontiguousarray(np.asarray(inputs["L0_attn_Wr_w"]).T))
    base["wi"] = _wblocks(np.ascontiguousarray(np.asarray(inputs["L0_attn_Wi_w"]).T))
    base["wo0"] = _wblocks(np.ascontiguousarray(np.asarray(inputs["L0_attn_out_w"]).T))
    base["wr_b"] = _feat(inputs["L0_attn_Wr_b"])
    base["wi_b"] = _feat(inputs["L0_attn_Wi_b"])
    base["cw0"] = _feat(np.asarray(inputs["L0_attn_conv_weight"]).reshape(D, KER), KER)
    base["cb0"] = _feat(inputs["L0_attn_conv_bias"])
    base["log_a"] = _feat(inputs["L0_attn_log_a"])
    base["norm_w"] = _feat(inputs["L0_attn_norm_w"])
    base["w1_0"] = _wblocks(np.asarray(inputs["L0_ffn_W1"]))
    base["b1_0"] = np.ascontiguousarray(
        np.asarray(inputs["L0_ffn_b1"]).reshape(HC, 128).T)
    base["w2_0"] = _wblocks(np.asarray(inputs["L0_ffn_W2"]))
    base["b2_0"] = _feat(inputs["L0_ffn_b2"])
    base["wip"] = _wblocks(np.ascontiguousarray(np.asarray(inputs["L1_attn_in_proj_w"]).T))
    base["cw1"] = _feat(np.asarray(inputs["L1_attn_conv_w"]).reshape(D, KER), KER)
    base["cb1"] = _feat(inputs["L1_attn_conv_b"])
    base["wxp"] = np.ascontiguousarray(
        np.asarray(inputs["L1_attn_x_proj_w"]).T.reshape(DC, 128, 33)
        .transpose(1, 0, 2).reshape(128, DC * 33))
    base["dtw"] = _feat(np.asarray(inputs["L1_attn_dt_proj_w"]).reshape(D))
    base["dtb"] = _feat(inputs["L1_attn_dt_proj_b"])
    base["alog"] = np.ascontiguousarray(np.asarray(inputs["L1_attn_A_log"]).reshape(DC, 128, NST))
    base["wop"] = _wblocks(np.ascontiguousarray(np.asarray(inputs["L1_attn_out_proj_w"]).T))
    base["w1_1"] = _wblocks(np.asarray(inputs["L1_ffn_w1"]))
    base["b1_1"] = np.ascontiguousarray(
        np.asarray(inputs["L1_ffn_b1"]).reshape(HC, 128).T)
    base["w2_1"] = _wblocks(np.asarray(inputs["L1_ffn_w2"]))
    base["b2_1"] = _feat(inputs["L1_ffn_b2"])

    consts = np.zeros((128, 4), np.float32)
    consts[:, 0] = 1.0
    consts[:, 1] = np.float32(2 ** 29)
    base["consts"] = consts
    base["bigrow"] = np.full((1, 512), np.float32(2 ** 29), np.float32)
    base["log_a_t"] = np.ascontiguousarray(
        np.asarray(inputs["L0_attn_log_a"]).reshape(DC, 128).T.astype(np.float32))

    def xslice(b, lo, hi):
        out = np.zeros((D, hi - lo + 3), np.float32)
        s = max(lo - 3, 0)
        out[:, 3 - (lo - s):] = x[b, s:hi].T
        return np.ascontiguousarray(out.reshape(DC, 128, hi - lo + 3))

    maps = []
    for c in range(8):
        b, h = c & 3, c >> 2
        m = dict(base)
        m["x_pre"] = xslice(b, 0, TP)
        m["x_own"] = m["x_pre"] if h == 0 else xslice(b, TP, 2 * TP)
        m["flag"] = np.full((128, 1), float(h), np.float32)
        maps.append(m)
    return maps


_PROG_CACHE = {}


def get_prog(dbg_names=(), dbg_phase=2):
    key = (tuple(dbg_names), dbg_phase)
    if key not in _PROG_CACHE:
        _PROG_CACHE[key] = Prog(dbg_names, dbg_phase).build()
    return _PROG_CACHE[key]


def kernel(**inputs):
    maps = stage_inputs(inputs)
    nc = get_prog()
    res = run_bass_kernel_spmd(nc, maps, core_ids=list(range(8)))
    out = np.empty((BATCH, L, D), np.float32)
    for c in range(8):
        b, h = c & 3, c >> 2
        o = res.results[c]["out"]
        out[b, h * TP:(h + 1) * TP, :] = o.reshape(D, TP).T
    return out


# version 30
# speedup vs baseline: 1.0312x; 1.0312x over previous
import numpy as np
from contextlib import ExitStack

import concourse.bass as bass
import concourse.mybir as mybir
from concourse.bass_utils import run_bass_kernel_spmd

F32 = mybir.dt.float32
BF16 = mybir.dt.bfloat16
ACT = mybir.ActivationFunctionType
ALU = mybir.AluOpType

D = 1024
DC = 8
BATCH = 4
L = 4096
TP = 2048
TC = 512
NTC = TP // TC
NST = 16
HID = 4096
HC = HID // 128
KER = 4

ENGINES = ("sync", "vector", "scalar", "tensor", "gpsimd")

DBG_TAPS = []
LIMIT_NTC = None
SKIP_P1 = False
STAGE_LIMIT = None


class Em:

    def __init__(self, nc, stk):
        self.nc = nc
        self.queues = {e: [] for e in ENGINES}
        self.sems = {}
        self.counts = {}
        for e in ENGINES:
            for kind in ("c", "d"):
                name = f"{e}_{kind}"
                self.sems[name] = stk.enter_context(nc.semaphore(name))
                self.counts[name] = 0
        for i in range(8):
            name = f"rg{i}"
            self.sems[name] = stk.enter_context(nc.semaphore(name))
            self.counts[name] = 0

    def wait(self, eng, *toks):
        best = {}
        for t in toks:
            if t is None:
                continue
            best[t[0]] = max(best.get(t[0], 0), t[1])
        for name, v in sorted(best.items()):
            if name == f"{eng}_c":
                continue
            sem = self.sems[name]
            self.queues[eng].append(lambda E, sem=sem, v=v: E.wait_ge(sem, v))

    def op(self, eng, fn, sig=False):
        if not sig:
            self.queues[eng].append(fn)
            return None
        name = f"{eng}_c"
        self.counts[name] += 1
        v = self.counts[name]
        sem = self.sems[name]

        def w(E, fn=fn, sem=sem):
            inst = fn(E)
            inst.then_inc(sem, 1)

        self.queues[eng].append(w)
        return (name, v)

    def sig_last(self, eng):
        return self.op(eng, lambda E: E.nop(), sig=True)

    def dma(self, eng, out, in_, serial=False, sem_name=None):
        name = sem_name or f"{eng}_d"
        prev = self.counts[name]
        if serial and prev:
            sem_p = self.sems[name]
            self.queues[eng].append(lambda E, s=sem_p, v=prev: E.wait_ge(s, v))
        self.counts[name] += 16
        v = self.counts[name]
        sem = self.sems[name]

        def w(E, out=out, in_=in_, sem=sem):
            E.dma_start(out=out, in_=in_).then_inc(sem, 16)

        self.queues[eng].append(w)
        return (name, v)

    def finalize(self, block):
        for e in ENGINES:
            dec = getattr(block, e)
            qq = self.queues[e]

            def body(E, qq=qq):
                for fn in qq:
                    fn(E)

            dec(body)


class Prog:
    def __init__(self, dbg_names=(), dbg_phase=2):
        self.stk = ExitStack()
        nc = bass.Bass("TRN2", target_bir_lowering=False, detect_race_conditions=False)
        self.nc = nc
        self.em = Em(nc, self.stk)
        self.pb = [self.stk.enter_context(nc.psum_tensor(f"pb{i}", [128, TC], F32))
                   for i in range(8)]
        self.pb_free = [None] * 8
        self.dbg_names = list(dbg_names)
        self.dbg_phase = dbg_phase
        self.out_toks = []
        self.chunk_guard = []

        d = {}
        dt_ = nc.dram_tensor
        d["x_pre"] = dt_("x_pre", [DC, 128, TP + 3], F32, kind="ExternalInput").ap()
        d["x_own"] = dt_("x_own", [DC, 128, TP + 3], F32, kind="ExternalInput").ap()
        d["flag"] = dt_("flag", [128, 1], F32, kind="ExternalInput").ap()
        d["wr"] = dt_("wr", [DC, 128, DC * 128], F32, kind="ExternalInput").ap()
        d["wi"] = dt_("wi", [DC, 128, DC * 128], F32, kind="ExternalInput").ap()
        d["wo0"] = dt_("wo0", [DC, 128, DC * 128], F32, kind="ExternalInput").ap()
        d["w1_0"] = dt_("w1_0", [HC, 128, DC * 128], F32, kind="ExternalInput").ap()
        d["w2_0"] = dt_("w2_0", [DC, 128, HC * 128], F32, kind="ExternalInput").ap()
        d["wip"] = dt_("wip", [2 * DC, 128, DC * 128], F32, kind="ExternalInput").ap()
        d["wop"] = dt_("wop", [DC, 128, DC * 128], F32, kind="ExternalInput").ap()
        d["w1_1"] = dt_("w1_1", [HC, 128, DC * 128], F32, kind="ExternalInput").ap()
        d["w2_1"] = dt_("w2_1", [DC, 128, HC * 128], F32, kind="ExternalInput").ap()
        d["wxp"] = dt_("wxp", [128, DC * 33], F32, kind="ExternalInput").ap()
        for nmv, cols in (("wr_b", 1), ("wi_b", 1), ("cw0", KER), ("cb0", 1),
                          ("log_a", 1), ("norm_w", 1), ("b2_0", 1),
                          ("cw1", KER), ("cb1", 1), ("dtw", 1), ("dtb", 1),
                          ("alog", NST), ("b2_1", 1)):
            d[nmv] = dt_(nmv, [DC, 128, cols], F32, kind="ExternalInput").ap()
        d["consts"] = dt_("consts", [128, 4], F32, kind="ExternalInput").ap()
        d["log_a_t"] = dt_("log_a_t", [128, DC], F32, kind="ExternalInput").ap()
        d["bigrow"] = dt_("bigrow", [1, 512], F32, kind="ExternalInput").ap()
        d["b1_0"] = dt_("b1_0", [128, HC], F32, kind="ExternalInput").ap()
        d["b1_1"] = dt_("b1_1", [128, HC], F32, kind="ExternalInput").ap()
        d["out"] = dt_("out", [DC, 128, TP], F32, kind="ExternalOutput").ap()
        d["ssm_hbm"] = dt_("ssm_hbm", [2, 33, TC], F32, kind="Internal").ap()
        self.wsplit = {}
        for wnm, mc_n, kc_n in (("wr", DC, DC), ("wi", DC, DC), ("wo0", DC, DC),
                                ("w1_0", HC, DC), ("w2_0", DC, HC),
                                ("wip", 2 * DC, DC), ("wop", DC, DC),
                                ("w1_1", HC, DC), ("w2_1", DC, HC)):
            hi = dt_(wnm + "_hi", [mc_n, 128, kc_n * 128], BF16, kind="Internal").ap()
            lo = dt_(wnm + "_lo", [mc_n, 128, kc_n * 128], BF16, kind="Internal").ap()
            self.wsplit[wnm] = (hi, lo, mc_n, kc_n)
        if self.dbg_names:
            d["dbg"] = dt_("dbg", [len(self.dbg_names), DC, 128, TP], F32,
                           kind="ExternalOutput").ap()
        self.d = d

    def sb(self, name, cols, parts=128):
        return self.stk.enter_context(self.nc.sbuf_tensor(name, [parts, cols], F32))

    def tap(self, name, tiles, tok, phase, tc):
        if name not in self.dbg_names or phase != self.dbg_phase:
            return None
        i = self.dbg_names.index(name)
        t = None
        for dc in range(DC):
            self.em.wait("gpsimd", tok)
            t = self.em.dma("gpsimd", self.d["dbg"][i, dc, :, tc * TC:(tc + 1) * TC],
                            tiles[dc][:, 0:TC])
            self.out_toks.append(t)
        return t

    def build(self):
        em = self.em
        nc = self.nc

        self.ones_col = self.sb("ones_col", 1)
        self.ones_row = self.stk.enter_context(nc.sbuf_tensor("ones_row", [1, 128], F32))
        self.row_m = self.stk.enter_context(nc.sbuf_tensor("row_m", [1, TC], F32))
        self.row_r = self.stk.enter_context(nc.sbuf_tensor("row_r", [1, TC], F32))
        self.row_t = self.stk.enter_context(nc.sbuf_tensor("row_t", [1, TC], F32))
        self.eps_ln = self.stk.enter_context(nc.sbuf_tensor("eps_ln", [1, 1], F32))
        self.eps_rms = self.stk.enter_context(nc.sbuf_tensor("eps_rms", [1, 1], F32))
        em.op("vector", lambda E: E.memset(self.ones_col[:], 1.0))
        em.op("vector", lambda E: E.memset(self.ones_row[:], 1.0))
        em.op("vector", lambda E: E.memset(self.eps_ln[:], 1e-5))
        self.init_tok = em.op("vector", lambda E: E.memset(self.eps_rms[:], 1e-6), sig=True)

        self.v = {}
        self.vtok = {}
        for nmv, cols in (("wr_b", 1), ("wi_b", 1), ("cw0", KER), ("cb0", 1),
                          ("log_a", 1), ("norm_w", 1), ("b2_0", 1),
                          ("cw1", KER), ("cb1", 1), ("dtw", 1), ("dtb", 1),
                          ("alog", NST), ("b2_1", 1)):
            t = self.sb("v_" + nmv, DC * cols)
            tk = None
            for dc in range(DC):
                tk = em.dma("sync", t[:, dc * cols:(dc + 1) * cols], self.d[nmv][dc])
            self.v[nmv] = t
            self.vtok[nmv] = tk
        for nmv in ("b1_0", "b1_1"):
            t = self.sb("v_" + nmv, HC)
            self.v[nmv] = t
            self.vtok[nmv] = em.dma("sync", t[:], self.d[nmv])
        self.flag = self.sb("flag_sb", 1)
        self.vtok["flag"] = em.dma("sync", self.flag[:], self.d["flag"])
        self.wxp = self.sb("wxp_sb", DC * 33)
        self.vtok["wxp"] = em.dma("sync", self.wxp[:], self.d["wxp"])
        self.vec_tok = self.vtok["flag"]

        self.Aneg = self.sb("Aneg", DC * NST)
        em.wait("scalar", self.vtok["alog"])
        self.A_tok0 = em.op("scalar", lambda E: E.activation(self.Aneg[:], self.v["alog"][:],
                                                             ACT.Exp, bias=0.0, scale=1.0),
                            sig=True)
        em.wait("vector", self.A_tok0)
        self.A_tok = em.op("vector", lambda E: E.tensor_scalar_mul(self.Aneg[:], self.Aneg[:],
                                                                   -1.0), sig=True)

        self.carry0 = self.sb("carry0", DC)
        self.carry1 = self.sb("carry1", DC * NST)
        self.halo1 = self.sb("halo1", DC * 3)
        self.st0 = self.sb("st0", DC)
        self.st1 = self.sb("st1", DC * NST)
        self.hal = self.sb("hal", DC * 3)

        self.NSLOTB = 80
        self.ringb = self.stk.enter_context(
            nc.sbuf_tensor("wringb", [128, self.NSLOTB * 128], BF16))
        self.ringb_free = [None] * self.NSLOTB
        self.ringb_head = 0

        self.bcB = self.sb("bcB", TC)
        self.bcC = self.sb("bcC", TC)
        self.bcS = self.sb("bcS", TC)

        self.ssm_slot = 0

        W512 = 512
        self.c_a = self.sb("c_a", DC)
        setup_scope = ExitStack()
        ssb = lambda nm: setup_scope.enter_context(
            self.nc.sbuf_tensor(nm, [128, W512], F32))
        self.gpad = ssb("gpad")
        self.gpadp = ssb("gpadp")
        self.gate = ssb("gate")
        self.capad = ssb("capad")
        self.cat1 = ssb("cat1")
        self.cat2 = ssb("cat2")
        big_bc = bass.AP(self.d["bigrow"].tensor, 0, [[0, 128], [1, W512]])
        gtok = em.dma("sync", self.gpad[:], big_bc)
        em.wait("vector", gtok, self.init_tok)
        em.op("vector", lambda E: E.tensor_scalar_add(self.gpadp[:], self.gpad[:], 1.0))
        self.gate_tok = em.op("vector", lambda E: E.tensor_sub(self.gate[:], self.gpadp[:],
                                                               self.gpad[:]), sig=True)

        def round32(v):
            em.op("vector", lambda E: E.tensor_tensor(self.cat1[:], v[:], self.gpadp[:],
                                                      ALU.mult))
            em.op("vector", lambda E: E.tensor_tensor(self.cat2[:], self.cat1[:], v[:],
                                                      ALU.subtract))
            em.op("vector", lambda E: E.tensor_tensor(self.cat1[:], self.cat1[:],
                                                      self.cat2[:], ALU.subtract))
            em.op("vector", lambda E: E.tensor_tensor(self.cat2[:], self.cat1[:], v[:],
                                                      ALU.subtract))
            return em.op("vector", lambda E: E.scalar_tensor_tensor(
                v[:], self.cat2[:], self.gate[:, 0:1], v[:], ALU.mult, ALU.add), sig=True)

        zt = em.op("vector", lambda E: E.memset(self.capad[:], 0.0), sig=True)
        em.wait("sync", zt)
        la_tok = em.dma("sync", self.capad[:, 0:DC], self.d["log_a_t"])
        em.wait("scalar", la_tok)
        e_tok = em.op("scalar", lambda E: E.activation(self.capad[:], self.capad[:],
                                                       ACT.Exp, bias=0.0, scale=-1.0),
                      sig=True)
        em.wait("vector", e_tok, self.gate_tok)
        round32(self.capad)
        em.op("vector", lambda E: E.tensor_scalar_add(self.capad[:], self.capad[:], 1.0))
        round32(self.capad)
        em.op("vector", lambda E: E.reciprocal(self.capad[:], self.capad[:]))
        ab_tok = round32(self.capad)
        em.wait("scalar", ab_tok)
        em.op("scalar", lambda E: E.activation(self.capad[:], self.capad[:],
                                               ACT.Ln, bias=0.0, scale=1.0))
        ca8_tok = em.op("scalar", lambda E: E.mul(self.capad[:], self.capad[:], 8.0),
                        sig=True)
        em.wait("vector", ca8_tok)
        self.ca_tok = em.op("vector", lambda E: E.tensor_copy(self.c_a[:], self.capad[:, 0:DC]),
                            sig=True)
        self.setup_scope = setup_scope


        self.prep_weights()
        self.chunk_guard = [self.prep_done]

        em.wait("sync", self.ca_tok)
        self.setup_scope.close()
        if not SKIP_P1:
            self.phase(1)
        else:
            em.op("vector", lambda E: E.memset(self.carry0[:], 0.0))
            em.op("vector", lambda E: E.memset(self.carry1[:], 0.0))
            self.ph1_tok = em.op("vector", lambda E: E.memset(self.halo1[:], 0.0), sig=True)
            self.scan0_tok = self.ph1_tok
            self.scan1_tok = self.ph1_tok
        self.phase(2)

        em.wait("gpsimd", self.out_toks[-1])
        em.op("gpsimd", lambda E: E.nop())

        with nc.Block() as block:
            em.finalize(block)
        self.stk.close()
        return nc


    def prep_weights(self):
        em = self.em
        nc = self.nc
        self.wready = {}
        self._wsync_done = set()
        self._ring_lane = 0
        order = ["wr", "wi", "wo0", "w1_0", "w2_0", "wip", "wop", "w1_1", "w2_1"]
        with ExitStack() as ps:
            buf = [ps.enter_context(nc.sbuf_tensor(f"prep{i}", [128, HC * 128], F32))
                   for i in range(2)]
            bhi = [ps.enter_context(nc.sbuf_tensor(f"prephi{i}", [128, HC * 128], BF16))
                   for i in range(2)]
            blo = [ps.enter_context(nc.sbuf_tensor(f"preplo{i}", [128, HC * 128], BF16))
                   for i in range(2)]
            free_tok = [None, None]
            i = 0
            for wnm in order:
                hi_t, lo_t, mc_n, kc_n = self.wsplit[wnm]
                cols = kc_n * 128
                toks = []
                for mc in range(mc_n):
                    b = i % 2
                    i += 1
                    em.wait("sync", free_tok[b])
                    ld = em.dma("sync", buf[b][:, 0:cols], self.d[wnm][mc],
                                serial=True)
                    em.wait("scalar", ld)
                    h_tok = em.op("scalar", lambda E, o=bhi[b][:, 0:cols],
                                  s=buf[b][:, 0:cols]:
                                  E.activation(o, s, ACT.Identity, bias=0.0, scale=1.0),
                                  sig=True)
                    em.wait("vector", h_tok)
                    l_tok = em.op("vector", lambda E, o=blo[b][:, 0:cols],
                                  s=buf[b][:, 0:cols], h2=bhi[b][:, 0:cols]:
                                  E.tensor_tensor(o, s, h2, ALU.subtract), sig=True)
                    em.wait("sync", l_tok)
                    t1 = em.dma("sync", hi_t[mc], bhi[b][:, 0:cols], serial=True)
                    t2 = em.dma("sync", lo_t[mc], blo[b][:, 0:cols], serial=True)
                    free_tok[b] = t2
                    toks.append(t2)
                self.wready[wnm] = toks[-1]
            self.prep_done = toks[-1]

    def wload(self, hbm_block_ap, n_tiles):
        runs = []
        need = n_tiles
        while need > 0:
            take = min(need, 16)
            if self.ring_head + take > self.NSLOT:
                self.ring_head = 0
            s0 = self.ring_head
            self.ring_head += take
            runs.append((s0, take))
            need -= take
        slots = []
        toks = []
        col0 = 0
        for s0, take in runs:
            for s in range(s0, s0 + take):
                self.em.wait("sync", self.ring_free[s])
            t = self.em.dma("sync", self.ring[:, s0 * 128:(s0 + take) * 128],
                            hbm_block_ap[:, col0 * 128:(col0 + take) * 128])
            toks.append(t)
            slots.extend(range(s0, s0 + take))
            col0 += take
        return slots, toks

    def mm_block(self, w_hbm_ap, n_k, rhs_tiles, rhs_tok, bank, extra=()):
        em = self.em
        slots, dtoks = self.wload(w_hbm_ap, n_k)
        pb = self.pb[bank]
        em.wait("tensor", rhs_tok, self.pb_free[bank], *dtoks, *extra)
        mm_tok = None
        for j, kc in enumerate(range(n_k)):
            s = slots[j]
            w_ap = self.ring[:, s * 128:(s + 1) * 128]
            mm_tok = em.op("tensor", lambda E, pb=pb, w=w_ap, r=rhs_tiles[kc],
                           a=(kc == 0), o=(kc == n_k - 1):
                           E.matmul(pb[:], w, r[:, 0:TC], start=a, stop=o),
                           sig=(kc == n_k - 1))
        for s in slots:
            self.ring_free[s] = mm_tok
        return mm_tok


    def wloadb(self, hbm_block_ap, n_tiles, lane=None):
        runs = []
        need = n_tiles
        while need > 0:
            take = min(need, 16)
            if self.ringb_head + take > self.NSLOTB:
                self.ringb_head = 0
            s0 = self.ringb_head
            self.ringb_head += take
            runs.append((s0, take))
            need -= take
        slots = []
        toks = []
        col0 = 0
        for s0, take in runs:
            for s in range(s0, s0 + take):
                self.em.wait("sync", self.ringb_free[s])
            t = self.em.dma("sync", self.ringb[:, s0 * 128:(s0 + take) * 128],
                            hbm_block_ap[:, col0 * 128:(col0 + take) * 128],
                            sem_name=lane)
            toks.append(t)
            slots.extend(range(s0, s0 + take))
            col0 += take
        return slots, toks

    def mm_block3(self, wnm, mc, n_k, rhs_hi, rhs_lo, rhs_tok, bank, extra=()):
        em = self.em
        hi_t, lo_t, _, _ = self.wsplit[wnm]
        pb = self.pb[bank]
        if wnm not in self._wsync_done:
            em.wait("sync", self.wready.get(wnm))
            self._wsync_done.add(wnm)
        rt_list = rhs_tok if isinstance(rhs_tok, list) else None
        em.wait("tensor", None if rt_list else rhs_tok, self.pb_free[bank],
                self.wready.get(wnm), *extra)
        mm_tok = None
        n_terms = 3 * n_k
        j = 0
        for k0 in range(0, n_k, 8):
            kn = min(8, n_k - k0)
            lane = f"rg{self._ring_lane}"
            self._ring_lane = (self._ring_lane + 1) % 8
            s_hi, d1 = self.wloadb(hi_t[mc][:, k0 * 128:(k0 + kn) * 128], kn,
                                   lane=lane)
            s_lo, d2 = self.wloadb(lo_t[mc][:, k0 * 128:(k0 + kn) * 128], kn,
                                   lane=lane)
            rt = ([rt_list[min(k0 + kn - 1, len(rt_list) - 1)]]
                  if rt_list else [])
            em.wait("tensor", *d1, *d2, *rt)
            for kc in range(kn):
                wh = self.ringb[:, s_hi[kc] * 128:(s_hi[kc] + 1) * 128]
                wl = self.ringb[:, s_lo[kc] * 128:(s_lo[kc] + 1) * 128]
                last_sub = (kc == kn - 1)
                for ti, (w_ap, r_ap) in enumerate(((wh, rhs_hi[k0 + kc]),
                                                   (wh, rhs_lo[k0 + kc]),
                                                   (wl, rhs_hi[k0 + kc]))):
                    mm_tok = em.op("tensor", lambda E, pb=pb, w=w_ap, r=r_ap,
                                   a=(j == 0), o=(j == n_terms - 1):
                                   E.matmul(pb[:], w, r[:, 0:TC], start=a, stop=o),
                                   sig=(last_sub and ti == 2))
                    j += 1
            for s in s_hi:
                self.ringb_free[s] = mm_tok
            for s in s_lo:
                self.ringb_free[s] = mm_tok
        return mm_tok

    def split(self, tset_fn, nm, tiles, tok, n=DC):
        em = self.em
        th = tset_fn(f"{nm}_hi", TC, BF16)
        tl = tset_fn(f"{nm}_lo", TC, BF16)
        toks = []
        for dc in range(n):
            em.wait("scalar", tok)
            hcast = em.op("scalar", lambda E, o=th[dc], s=tiles[dc]:
                          E.activation(o[:], s[:, 0:TC], ACT.Identity,
                                       bias=0.0, scale=1.0), sig=True)
            em.wait("vector", hcast)
            toks.append(em.op("vector",
                              lambda E, o=tl[dc], s=tiles[dc], h2=th[dc]:
                              E.tensor_tensor(o[:], s[:, 0:TC], h2[:],
                                              ALU.subtract), sig=True))
        return th, tl, toks

    def keeper(self, tok):
        em = self.em
        em.wait("tensor", tok)
        em.op("tensor", lambda E: E.matmul(self.pb[4][64:65, 0:1],
                                           self.ones_row[0:1, 0:1],
                                           self.ones_col[0:1, 0:1],
                                           start=True, stop=True))

    def bcast_row_psum(self, row_ap, bank, tok):
        em = self.em
        pb = self.pb[bank]
        em.wait("tensor", tok, self.pb_free[bank])
        return em.op("tensor", lambda E: E.matmul(pb[:], self.ones_row[:], row_ap,
                                                  start=True, stop=True), sig=True)

    def ln_stats(self, y, y_tok, sq, bank_s, bank_q, eps_ap, want_mean):
        em = self.em
        pbs, pbq = self.pb[bank_s], self.pb[bank_q]
        sq_tok = None
        for dc in range(DC):
            em.wait("scalar", y_tok)
            sq_tok = em.op("scalar", lambda E, o=sq[dc], i=y[dc]:
                           E.activation(o[:, 0:TC], i[:, 0:TC], ACT.Square,
                                        bias=0.0, scale=1.0), sig=(dc == DC - 1))
        em.wait("tensor", y_tok, sq_tok, self.pb_free[bank_s], self.pb_free[bank_q])
        if want_mean:
            for dc in range(DC):
                em.op("tensor", lambda E, r=y[dc], a=(dc == 0), o=(dc == DC - 1):
                      E.matmul(pbs[0:1, :], self.ones_col[:], r[:, 0:TC],
                               start=a, stop=o))
        st_tok = None
        for dc in range(DC):
            st_tok = em.op("tensor", lambda E, r=sq[dc], a=(dc == 0), o=(dc == DC - 1):
                           E.matmul(pbq[0:1, :], self.ones_col[:], r[:, 0:TC],
                                    start=a, stop=o), sig=(dc == DC - 1))
        em.wait("vector", st_tok)
        if want_mean:
            em.op("vector", lambda E: E.tensor_scalar_mul(self.row_m[:], pbs[0:1, :], 1.0 / D))
            em.op("vector", lambda E: E.tensor_scalar_mul(self.row_r[:], pbq[0:1, :], 1.0 / D))
            em.op("vector", lambda E: E.tensor_tensor(self.row_t[:], self.row_m[:],
                                                      self.row_m[:], ALU.mult))
            v_tok = em.op("vector", lambda E: E.tensor_sub(self.row_r[:], self.row_r[:],
                                                           self.row_t[:]), sig=True)
        else:
            v_tok = em.op("vector", lambda E: E.tensor_scalar_mul(self.row_r[:], pbq[0:1, :],
                                                                  1.0 / D), sig=True)
        em.wait("scalar", v_tok)
        s_tok = em.op("scalar", lambda E: E.activation(self.row_r[:], self.row_r[:],
                                                       ACT.Sqrt, bias=eps_ap, scale=1.0),
                      sig=True)
        em.wait("vector", s_tok)
        if want_mean:
            em.op("vector", lambda E: E.reciprocal(self.row_r[:], self.row_r[:]))
            out_tok = em.op("vector", lambda E: E.tensor_tensor(self.row_t[:], self.row_m[:],
                                                                self.row_r[:], ALU.mult),
                            sig=True)
        else:
            out_tok = em.op("vector", lambda E: E.reciprocal(self.row_r[:], self.row_r[:]),
                            sig=True)
        self.pb_free[bank_s] = out_tok
        self.pb_free[bank_q] = out_tok
        return out_tok

    def layernorm(self, y, y_tok, out, sq):
        em = self.em
        st = self.ln_stats(y, y_tok, sq, 4, 5, self.eps_ln[0:1, 0:1], True)
        b_r = self.bcast_row_psum(self.row_r[:], 6, st)
        b_mr = self.bcast_row_psum(self.row_t[:], 7, st)
        pbr, pbm = self.pb[6], self.pb[7]
        o_tok = None
        for dc in range(DC):
            em.wait("vector", b_r, b_mr)
            em.op("vector", lambda E, o=out[dc], i=y[dc]:
                  E.tensor_tensor(o[:, 0:TC], i[:, 0:TC], pbr[:], ALU.mult))
            o_tok = em.op("vector", lambda E, o=out[dc]:
                          E.tensor_tensor(o[:, 0:TC], o[:, 0:TC], pbm[:], ALU.subtract),
                          sig=(dc == DC - 1))
        self.pb_free[6] = o_tok
        self.pb_free[7] = o_tok
        return o_tok

    def phase(self, phase):
        em = self.em
        is_p1 = (phase == 1)
        xin_hbm = self.d["x_pre"] if is_p1 else self.d["x_own"]

        if is_p1:
            em.wait("vector", self.init_tok)
            em.op("vector", lambda E: E.memset(self.st0[:], 0.0))
            em.op("vector", lambda E: E.memset(self.st1[:], 0.0))
            self.state_tok = em.op("vector", lambda E: E.memset(self.hal[:], 0.0), sig=True)
        else:
            em.wait("vector", self.ph1_tok, self.vtok["flag"])
            em.op("vector", lambda E: E.tensor_scalar(self.st0[:], self.carry0[:],
                                                      self.flag[:, 0:1], None, ALU.mult))
            em.op("vector", lambda E: E.tensor_scalar(self.st1[:], self.carry1[:],
                                                      self.flag[:, 0:1], None, ALU.mult))
            self.state_tok = em.op("vector", lambda E:
                                   E.tensor_scalar(self.hal[:], self.halo1[:],
                                                   self.flag[:, 0:1], None, ALU.mult),
                                   sig=True)

        for tc in range(LIMIT_NTC or NTC):
            self.chunk(phase, tc, xin_hbm)

        if is_p1:
            if not hasattr(self, 'scan1_tok'):
                self.scan1_tok = self.scan0_tok if hasattr(self, 'scan0_tok') else None
            if not hasattr(self, 'scan0_tok'):
                self.scan0_tok = None
            em.wait("vector", self.scan0_tok, self.scan1_tok)
            em.op("vector", lambda E: E.tensor_copy(self.carry0[:], self.st0[:]))
            em.op("vector", lambda E: E.tensor_copy(self.carry1[:], self.st1[:]))
            self.ph1_tok = em.op("vector", lambda E: E.tensor_copy(self.halo1[:], self.hal[:]),
                                 sig=True)

    def chunk(self, phase, tc, xin_hbm):
        em = self.em
        nc = self.nc
        is_p1 = (phase == 1)
        nm = f"p{phase}t{tc}"

        with ExitStack() as sc:
            def tset(name, cols=TC, dtype=F32):
                t = sc.enter_context(nc.sbuf_tensor(f"{nm}_{name}", [128, DC * cols], dtype))
                return [t[:, i * cols:(i + 1) * cols] for i in range(DC)]

            em.wait("sync", *self.chunk_guard)
            em.wait("gpsimd", *self.chunk_guard)
            rs = tset("rs")

            self.l0_attn(nm, sc, tset, xin_hbm, rs, phase, tc)
            if STAGE_LIMIT in (None, 'ffn', 'l1'):
                self.ffn(nm + "f0", rs, self.rs_tok, "w1_0", "b1_0", "w2_0", "b2_0", phase, tc)
            if STAGE_LIMIT in (None, 'l1'):
                self.l1_attn(nm, sc, tset, rs, self.rs_tok, phase, tc)
            if not is_p1 and STAGE_LIMIT is None:
                self.ffn(nm + "f1", rs, self.rs_tok, "w1_1", "b1_1", "w2_1", "b2_1", phase, tc)
                self.tap("out", rs, self.rs_tok, phase, tc)
                for dc in range(DC):
                    em.wait("gpsimd", self.rs_tok)
                    t = em.dma("gpsimd", self.d["out"][dc, :, tc * TC:(tc + 1) * TC],
                               rs[dc][:, 0:TC])
                    self.out_toks.append(t)
            elif not is_p1:
                tiles = getattr(self, 'dump_tiles', None) or rs
                for dc in range(DC):
                    em.wait("gpsimd", self.rs_tok)
                    t = em.dma("gpsimd", self.d["out"][dc, :, tc * TC:(tc + 1) * TC],
                               tiles[dc][:, 0:TC])
                    self.out_toks.append(t)
            ks = getattr(self, 'keep_scope', None)
            if ks is not None:
                ks.close()
                self.keep_scope = None
                self.dump_tiles = None
            self.chunk_guard = [self.em.tokq("vector"), self.em.tokq("scalar"),
                                self.em.tokq("tensor"), self.em.tokq("gpsimd")]

    def l0_attn(self, nm, sc_outer, tset_outer, xin_hbm, rs, phase, tc):
        em = self.em
        nc = self.nc
        v = self.v
        lsc = ExitStack()
        def tset(name, cols=TC, dtype=F32):
            t = lsc.enter_context(nc.sbuf_tensor(f"{nm}0_{name}", [128, DC * cols], dtype))
            return [t[:, i * cols:(i + 1) * cols] for i in range(DC)]
        x = tset("x", TC + 3)
        x_tok = None
        for dc in range(DC):
            x_tok = em.dma("sync", x[dc], xin_hbm[dc, :, tc * TC: tc * TC + TC + 3])
        xc = tset("xc")
        c_tok = None
        for dc in range(DC):
            em.wait("vector", x_tok, self.vtok["cb0"])
            em.op("vector", lambda E, o=xc[dc], i=x[dc], dc=dc:
                  E.tensor_scalar(o[:], i[:, 3:3 + TC],
                                  v["cw0"][:, dc * KER + KER - 1:dc * KER + KER],
                                  v["cb0"][:, dc:dc + 1], ALU.mult, ALU.add))
            for k in range(KER - 1):
                c_tok = em.op("vector", lambda E, o=xc[dc], i=x[dc], dc=dc, k=k:
                              E.scalar_tensor_tensor(o[:], i[:, k:k + TC],
                                                     v["cw0"][:, dc * KER + k:dc * KER + k + 1],
                                                     o[:], ALU.mult, ALU.add),
                              sig=(dc == DC - 1 and k == KER - 2))
        self.tap("xc0", xc, c_tok, phase, tc)
        if STAGE_LIMIT == 'conv':
            self.rs_tok = c_tok
            self.dump_tiles = xc
            self.scan0_tok = c_tok
            self.keep_scope = lsc
            return
        xch, xcl, xcsp_tok = self.split(tset, "xcs0", xc, c_tok)
        r = tset("r")
        ii = tset("i")
        for mc in range(DC):
            bank = mc % 2
            mm = self.mm_block3("wr", mc, DC, xch, xcl, xcsp_tok, bank)
            em.wait("scalar", mm, self.vtok["wr_b"])
            e1 = em.op("scalar", lambda E, o=r[mc], pb=self.pb[bank], mc=mc:
                       E.activation(o[:], pb[:], ACT.Sigmoid,
                                    bias=v["wr_b"][:, mc:mc + 1], scale=1.0), sig=True)
            self.pb_free[bank] = e1
            bank = 2 + mc % 2
            mm = self.mm_block3("wi", mc, DC, xch, xcl, xcsp_tok, bank)
            em.wait("scalar", mm, self.vtok["wi_b"])
            e2 = em.op("scalar", lambda E, o=ii[mc], pb=self.pb[bank], mc=mc:
                       E.activation(o[:], pb[:], ACT.Sigmoid,
                                    bias=v["wi_b"][:, mc:mc + 1], scale=1.0), sig=True)
            self.pb_free[bank] = e2
        ri_tok = e2
        rg_t = self.tap("rg", r, ri_tok, phase, tc)
        ig_t = self.tap("ig", ii, ri_tok, phase, tc)
        if STAGE_LIMIT == 'gates':
            self.rs_tok = ri_tok
            self.dump_tiles = r
            self.scan0_tok = ri_tok
            self.keep_scope = lsc
            return
        a = r
        a_tok = None
        for dc in range(DC):
            em.wait("scalar", self.ca_tok, rg_t)
            a_tok = em.op("scalar", lambda E, o=a[dc], dc=dc:
                          E.activation(o[:], o[:], ACT.Exp, bias=0.0,
                                       scale=self.c_a[:, dc:dc + 1]), sig=(dc == DC - 1))
        b = tset("b")
        b_tok = None
        for dc in range(DC):
            em.wait("vector", a_tok)
            em.op("vector", lambda E, t=b[dc], i=a[dc]:
                  E.tensor_tensor(t[:], i[:], i[:], ALU.mult))
            em.op("vector", lambda E, t=b[dc]:
                  E.tensor_scalar(t[:], t[:], -1.0, 1.0, ALU.mult, ALU.add))
            tv = em.op("vector", lambda E, t=b[dc]:
                       E.tensor_scalar_max(t[:], t[:], 1e-6), sig=True)
            em.wait("scalar", tv)
            s_tok = em.op("scalar", lambda E, t=b[dc]:
                          E.activation(t[:], t[:], ACT.Sqrt, bias=0.0, scale=1.0), sig=True)
            em.wait("vector", s_tok)
            em.op("vector", lambda E, t=b[dc], i=ii[dc]:
                  E.tensor_tensor(t[:], t[:], i[:], ALU.mult))
            b_tok = em.op("vector", lambda E, t=b[dc], c=xc[dc]:
                          E.tensor_tensor(t[:], t[:], c[:], ALU.mult),
                          sig=(dc == DC - 1))
        aa_t = self.tap("aa", a, a_tok, phase, tc)
        bb_t = self.tap("bb", b, b_tok, phase, tc)
        h = ii
        sc_tok = None
        em.wait("vector", b_tok, self.state_tok, ig_t, aa_t, bb_t)
        for dc in range(DC):
            t = em.op("vector", lambda E, o=h[dc], A=a[dc], Bb=b[dc], dc=dc:
                      E.tensor_tensor_scan(o[:], A[:], Bb[:], self.st0[:, dc:dc + 1],
                                           ALU.mult, ALU.add), sig=True)
            self.keeper(t)
        for dc in range(DC):
            sc_tok = em.op("vector", lambda E, o=h[dc], dc=dc:
                           E.tensor_copy(self.st0[:, dc:dc + 1], o[:, TC - 1:TC]),
                           sig=(dc == DC - 1))
        self.scan0_tok = sc_tok
        self.tap("h0", h, sc_tok, phase, tc)
        if STAGE_LIMIT == 'scan':
            self.rs_tok = sc_tok
            self.dump_tiles = h
            self.keep_scope = lsc
            return
        hsp_h, hsp_l, hsp_tok = self.split(tset, "hsp", h, sc_tok)
        o = a
        sq = b
        ev_tok = None
        for mc in range(DC):
            bank = mc % 2
            mm = self.mm_block3("wo0", mc, DC, hsp_h, hsp_l, hsp_tok, bank)
            em.wait("scalar", mm)
            em.op("scalar", lambda E, o2=o[mc], pb=self.pb[bank]:
                  E.activation(o2[:], pb[:], ACT.Identity, bias=0.0, scale=1.0))
            ev_tok = em.op("scalar", lambda E, s=sq[mc], pb=self.pb[bank]:
                           E.activation(s[:], pb[:], ACT.Square, bias=0.0, scale=1.0),
                           sig=True)
            self.pb_free[bank] = ev_tok
        st = self.ln_stats(o, ev_tok, sq, 4, 5, self.eps_rms[0:1, 0:1], False)
        bc = self.bcast_row_psum(self.row_r[:], 6, st)
        y1 = h
        y_tok = None
        for dc in range(DC):
            em.wait("vector", bc, self.vtok["norm_w"])
            em.op("vector", lambda E, t=o[dc]:
                  E.tensor_tensor(t[:], t[:], self.pb[6][:], ALU.mult))
            y_tok = em.op("vector", lambda E, y=y1[dc], t=o[dc], xx=x[dc], dc=dc:
                          E.scalar_tensor_tensor(y[:], t[:], v["norm_w"][:, dc:dc + 1],
                                                 xx[:, 3:3 + TC], ALU.mult, ALU.add),
                          sig=(dc == DC - 1))
        self.pb_free[6] = y_tok
        self.tap("y1", y1, y_tok, phase, tc)
        self.rs_tok = self.layernorm(y1, y_tok, rs, sq)
        self.tap("ln1", rs, self.rs_tok, phase, tc)
        lsc.close()

    def ffn(self, nm, rs, rs_tok, w1, b1, w2, b2, phase, tc):
        em = self.em
        nc = self.nc
        v = self.v
        with ExitStack() as fsc:
            def ftset(name, cols=TC, dtype=F32, n=DC):
                t = fsc.enter_context(nc.sbuf_tensor(f"{nm}_{name}", [128, n * cols], dtype))
                return [t[:, i * cols:(i + 1) * cols] for i in range(n)]
            rsh, rsl, rsp_tok = self.split(ftset, "rsp", rs, rs_tok)
            hid_h = ftset("hidh", TC, BF16, HC)
            hid_l = ftset("hidl", TC, BF16, HC)
            y2 = ftset("y2")
            sqs = ftset("sq")
            hft = ftset("hft", TC, F32, 2)
            hft_guard = [None, None]
            h_tok = None
            h_toks = []
            for hc in range(HC):
                bank = hc % 2
                mm = self.mm_block3(w1, hc, DC, rsh, rsl, rsp_tok, bank)
                em.wait("scalar", mm, self.vtok[b1], hft_guard[hc % 2])
                hf_tok = em.op("scalar", lambda E, o=hft[hc % 2], pb=self.pb[bank], hc=hc:
                               E.activation(o[:], pb[:], ACT.Relu,
                                            bias=v[b1][:, hc:hc + 1], scale=1.0), sig=True)
                self.pb_free[bank] = hf_tok
                hh_tok = em.op("scalar", lambda E, o=hid_h[hc], s=hft[hc % 2]:
                               E.activation(o[:], s[:], ACT.Identity,
                                            bias=0.0, scale=1.0), sig=True)
                em.wait("vector", hh_tok)
                h_tok = em.op("vector", lambda E, o=hid_l[hc], s=hft[hc % 2], h2=hid_h[hc]:
                              E.tensor_tensor(o[:], s[:], h2[:], ALU.subtract),
                              sig=True)
                hft_guard[hc % 2] = h_tok
                h_toks.append(h_tok)
            e_tok = None
            for oc in range(DC):
                bank = 2 + oc % 2
                mm = self.mm_block3(w2, oc, HC, hid_h, hid_l, h_toks, bank)
                em.wait("scalar", mm, self.vtok[b2])
                ev = em.op("scalar", lambda E, o=y2[oc], pb=self.pb[bank], oc=oc:
                           E.activation(o[:], pb[:], ACT.Identity,
                                        bias=v[b2][:, oc:oc + 1], scale=1.0), sig=True)
                self.pb_free[bank] = ev
                em.wait("vector", ev)
                e_tok = em.op("vector", lambda E, o=y2[oc], r=rs[oc]:
                              E.tensor_add(o[:], o[:], r[:]), sig=(oc == DC - 1))
            self.rs_tok = self.layernorm(y2, e_tok, rs, sqs)
            self.tap("ffn" + nm[-2:], rs, self.rs_tok, phase, tc)

    def l1_attn(self, nm, sc_outer, tset_outer, rs, rs_tok, phase, tc):
        em = self.em
        nc = self.nc
        v = self.v
        is_p1 = (phase == 1)
        with ExitStack() as lsc:
            def tset(name, cols=TC, dtype=F32):
                t = lsc.enter_context(nc.sbuf_tensor(f"{nm}L_{name}", [128, DC * cols], dtype))
                return [t[:, i * cols:(i + 1) * cols] for i in range(DC)]

            xin = tset("xin", TC + 3)
            z = tset("z") if not is_p1 else None
            with ExitStack() as rsc:
                def rtset(name, cols=TC, dtype=F32):
                    t = rsc.enter_context(
                        nc.sbuf_tensor(f"{nm}Lr_{name}", [128, DC * cols], dtype))
                    return [t[:, i * cols:(i + 1) * cols] for i in range(DC)]
                rsh, rsl, rsp_tok = self.split(rtset, "rsp1", rs, rs_tok)
                ip_tok = None
                for mc in range(DC):
                    bank = mc % 2
                    mm = self.mm_block3("wip", mc, DC, rsh, rsl, rsp_tok, bank)
                    em.wait("scalar", mm)
                    ip_tok = em.op("scalar", lambda E, o=xin[mc], pb=self.pb[bank]:
                                   E.activation(o[:, 3:3 + TC], pb[:], ACT.Identity,
                                                bias=0.0, scale=1.0), sig=True)
                    self.pb_free[bank] = ip_tok
                if not is_p1:
                    for mc in range(DC):
                        bank = 2 + mc % 2
                        mm = self.mm_block3("wip", DC + mc, DC, rsh, rsl, rsp_tok, bank)
                        em.wait("scalar", mm)
                        zt = em.op("scalar", lambda E, o=z[mc], pb=self.pb[bank]:
                                   E.activation(o[:], pb[:], ACT.Identity,
                                                bias=0.0, scale=1.0), sig=True)
                        self.pb_free[bank] = zt
                self.rsp1_guard = ip_tok if is_p1 else zt
            hl_tok = None
            for dc in range(DC):
                em.wait("gpsimd", ip_tok, self.state_tok if tc == 0 else self.halo_tok)
                hl_tok = em.op("gpsimd", lambda E, o=xin[dc], dc=dc:
                               E.tensor_copy(o[:, 0:3], self.hal[:, dc * 3:dc * 3 + 3]),
                               sig=(dc == DC - 1))
            nh_tok = None
            for dc in range(DC):
                em.wait("vector", ip_tok, hl_tok)
                nh_tok = em.op("vector", lambda E, i=xin[dc], dc=dc:
                               E.tensor_copy(self.hal[:, dc * 3:dc * 3 + 3],
                                             i[:, TC:TC + 3]),
                               sig=(dc == DC - 1))
            self.halo_tok = nh_tok
            xcs = tset("xcs")
            cs_tok = None
            for dc in range(DC):
                em.wait("vector", hl_tok, self.vtok["cb1"], self.rsp1_guard)
                em.op("vector", lambda E, o=xcs[dc], i=xin[dc], dc=dc:
                      E.tensor_scalar(o[:], i[:, 3:3 + TC],
                                      v["cw1"][:, dc * KER + KER - 1:dc * KER + KER],
                                      v["cb1"][:, dc:dc + 1], ALU.mult, ALU.add))
                for k in range(KER - 1):
                    cs_tok = em.op("vector", lambda E, o=xcs[dc], i=xin[dc], dc=dc, k=k:
                                   E.scalar_tensor_tensor(o[:], i[:, k:k + TC],
                                                          v["cw1"][:, dc * KER + k:dc * KER + k + 1],
                                                          o[:], ALU.mult, ALU.add),
                                   sig=(dc == DC - 1 and k == KER - 2))
            sl_tok = None
            for dc in range(DC):
                em.wait("scalar", cs_tok)
                sl_tok = em.op("scalar", lambda E, o=xcs[dc]:
                               E.activation(o[:], o[:], ACT.Silu, bias=0.0, scale=1.0),
                               sig=(dc == DC - 1))
            self.tap("xcs", xcs, sl_tok, phase, tc)
            pb4 = self.pb[4]
            em.wait("tensor", sl_tok, self.pb_free[4], self.vtok["wxp"])
            xp_tok = None
            for kc in range(DC):
                xp_tok = em.op("tensor", lambda E, kc=kc:
                               E.matmul(pb4[0:33, :], self.wxp[:, kc * 33:(kc + 1) * 33],
                                        xcs[kc][:, 0:TC], start=(kc == 0),
                                        stop=(kc == DC - 1)), sig=(kc == DC - 1))
            self.xp_done = xp_tok
            ssm = lsc.enter_context(nc.sbuf_tensor(f"{nm}L_ssm", [33, TC], F32))
            em.wait("vector", xp_tok)
            ssm_tok = em.op("vector", lambda E: E.tensor_copy(ssm[:], pb4[0:33, :]), sig=True)
            self.pb_free[4] = ssm_tok
            slot = self.ssm_slot
            self.ssm_slot ^= 1
            em.wait("sync", ssm_tok, getattr(self, f"_ssm_guard{slot}", None))
            ssm_hbm_tok = em.dma("sync", self.d["ssm_hbm"][slot], ssm[:])
            em.wait("sync", ssm_hbm_tok)
            row = self.d["ssm_hbm"].tensor
            def hrow(r):
                off = (slot * 33 + r) * TC
                return bass.AP(row, off, [[0, 128], [1, TC]])
            s_tok = em.dma("sync", self.bcS[:], hrow(32))
            dt = tset("dt")
            dts_t = lsc.enter_context(nc.sbuf_tensor(f"{nm}L_dts", [128, 2 * TC], F32))
            dts = [dts_t[:, 0:TC], dts_t[:, TC:2 * TC]]
            dts_guard = [None, None]
            dtx = xcs
            d_tok = None
            for dc in range(DC):
                em.wait("scalar", s_tok, self.vtok["dtb"])
                em.op("scalar", lambda E, o=dt[dc], dc=dc:
                      E.activation(o[:], self.bcS[:], ACT.Abs,
                                   bias=v["dtb"][:, dc:dc + 1], scale=v["dtw"][:, dc:dc + 1]))
                em.op("scalar", lambda E, o=dt[dc]:
                      E.activation(o[:], o[:], ACT.Exp, bias=0.0, scale=-1.0))
                em.op("scalar", lambda E, o=dt[dc]:
                      E.activation(o[:], o[:], ACT.Ln,
                                   bias=self.ones_col[:, 0:1], scale=1.0))
                em.wait("scalar", dts_guard[dc % 2])
                s2 = em.op("scalar", lambda E, o=dts[dc % 2], dc=dc:
                           E.activation(o[:], self.bcS[:], ACT.Relu,
                                        bias=v["dtb"][:, dc:dc + 1],
                                        scale=v["dtw"][:, dc:dc + 1]), sig=True)
                em.wait("vector", s2)
                d_tok = em.op("vector", lambda E, o=dt[dc], p=dts[dc % 2]:
                              E.tensor_add(o[:], o[:], p[:]), sig=True)
                dts_guard[dc % 2] = d_tok
            self.tap("dt", dt, d_tok, phase, tc)
            dx_tok = None
            for dc in range(DC):
                em.wait("vector", d_tok, sl_tok, self.xp_done)
                dx_tok = em.op("vector", lambda E, o=dtx[dc], a2=dt[dc]:
                               E.tensor_tensor(o[:], a2[:], o[:], ALU.mult),
                               sig=(dc == DC - 1))
            dA = tset("dA")
            dBx = tset("dBx")
            hh = tset("hh")
            y = tset("y") if not is_p1 else None
            ya_tok = None
            scn_tok = None
            prev_hc_tok = None
            prev_bx = None
            for n in range(NST):
                em.wait("sync", prev_bx, prev_hc_tok)
                b_tok = em.dma("sync", self.bcB[:], hrow(n))
                c_tok = em.dma("sync", self.bcC[:], hrow(NST + n)) if not is_p1 else None
                aw = None
                for dc in range(DC):
                    em.wait("scalar", d_tok, self.A_tok, scn_tok)
                    aw = em.op("scalar", lambda E, o=dA[dc], i=dt[dc], dc=dc, n=n:
                               E.activation(o[:], i[:], ACT.Exp, bias=0.0,
                                            scale=self.Aneg[:, dc * NST + n:dc * NST + n + 1]),
                               sig=(dc == DC - 1))
                bx = None
                for dc in range(DC):
                    em.wait("vector", b_tok, dx_tok, prev_hc_tok or scn_tok, ya_tok)
                    bx = em.op("vector", lambda E, o=dBx[dc], a2=dtx[dc]:
                               E.tensor_tensor(o[:], a2[:], self.bcB[:], ALU.mult),
                               sig=(dc == DC - 1))
                prev_bx = bx
                em.wait("vector", aw, bx, self.state_tok)
                for dc in range(DC):
                    col = dc * NST + n
                    t = em.op("vector", lambda E, o=hh[dc], A=dA[dc], Bb=dBx[dc], col=col:
                              E.tensor_tensor_scan(o[:], A[:], Bb[:],
                                                   self.st1[:, col:col + 1],
                                                   ALU.mult, ALU.add), sig=True)
                    self.keeper(t)
                for dc in range(DC):
                    col = dc * NST + n
                    scn_tok = em.op("vector", lambda E, o=hh[dc], col=col:
                                    E.tensor_copy(self.st1[:, col:col + 1], o[:, TC - 1:TC]),
                                    sig=(dc == DC - 1))
                if not is_p1:
                    hc_tok = None
                    for dc in range(DC):
                        em.wait("vector", scn_tok, c_tok, ya_tok)
                        hc_tok = em.op("vector", lambda E, o=dBx[dc], a2=hh[dc]:
                                       E.tensor_tensor(o[:], a2[:], self.bcC[:], ALU.mult),
                                       sig=(dc == DC - 1))
                    prev_hc_tok = hc_tok
                    for dc in range(DC):
                        em.wait("gpsimd", hc_tok)
                        if n == 0:
                            ya_tok = em.op("gpsimd", lambda E, o=y[dc], a2=dBx[dc]:
                                           E.tensor_copy(o[:], a2[:]),
                                           sig=(dc == DC - 1))
                        else:
                            ya_tok = em.op("gpsimd", lambda E, o=y[dc], a2=dBx[dc]:
                                           E.tensor_add(o[:], o[:], a2[:]),
                                           sig=(dc == DC - 1))
            self.scan1_tok = scn_tok
            setattr(self, f"_ssm_guard{slot}", scn_tok if is_p1 else ya_tok)
            if is_p1:
                self.rs_tok = rs_tok
                return
            self.tap("ymam", y, ya_tok, phase, tc)
            zs_tok = None
            for dc in range(DC):
                em.wait("scalar", zt)
                zs_tok = em.op("scalar", lambda E, o=z[dc]:
                               E.activation(o[:], o[:], ACT.Silu, bias=0.0, scale=1.0),
                               sig=(dc == DC - 1))
            ym_tok = None
            for dc in range(DC):
                em.wait("vector", zs_tok, ya_tok)
                ym_tok = em.op("vector", lambda E, o=y[dc], a2=z[dc]:
                               E.tensor_tensor(o[:], o[:], a2[:], ALU.mult),
                               sig=(dc == DC - 1))
            y3 = dt
            sqt = dA
            ysh, ysl, ysp_tok = self.split(tset, "ysp", y, ym_tok)
            e_tok = None
            for mc in range(DC):
                bank = mc % 2
                mm = self.mm_block3("wop", mc, DC, ysh, ysl, ysp_tok, bank)
                em.wait("scalar", mm)
                ev = em.op("scalar", lambda E, o=y3[mc], pb=self.pb[bank]:
                           E.activation(o[:], pb[:], ACT.Identity, bias=0.0, scale=1.0),
                           sig=True)
                self.pb_free[bank] = ev
                em.wait("vector", ev)
                e_tok = em.op("vector", lambda E, o=y3[mc], r2=rs[mc]:
                              E.tensor_add(o[:], o[:], r2[:]), sig=(mc == DC - 1))
            self.rs_tok = self.layernorm(y3, e_tok, rs, sqt)
            self.tap("ln3", rs, self.rs_tok, phase, tc)


def _tokq(self, eng, kind="c"):
    name = f"{eng}_{kind}"
    return (name, self.counts[name])


Em.tokq = _tokq



def _wblocks(wt):
    K, M = wt.shape
    return np.ascontiguousarray(
        wt.reshape(K // 128, 128, M // 128, 128).transpose(2, 1, 0, 3)
        .reshape(M // 128, 128, (K // 128) * 128))


def _feat(vec, cols=1):
    return np.ascontiguousarray(np.asarray(vec).reshape(DC, 128, cols))


def stage_inputs(inputs):
    x = np.asarray(inputs["L0_attn_x"])
    base = {}
    base["wr"] = _wblocks(np.ascontiguousarray(np.asarray(inputs["L0_attn_Wr_w"]).T))
    base["wi"] = _wblocks(np.ascontiguousarray(np.asarray(inputs["L0_attn_Wi_w"]).T))
    base["wo0"] = _wblocks(np.ascontiguousarray(np.asarray(inputs["L0_attn_out_w"]).T))
    base["wr_b"] = _feat(inputs["L0_attn_Wr_b"])
    base["wi_b"] = _feat(inputs["L0_attn_Wi_b"])
    base["cw0"] = _feat(np.asarray(inputs["L0_attn_conv_weight"]).reshape(D, KER), KER)
    base["cb0"] = _feat(inputs["L0_attn_conv_bias"])
    base["log_a"] = _feat(inputs["L0_attn_log_a"])
    base["norm_w"] = _feat(inputs["L0_attn_norm_w"])
    base["w1_0"] = _wblocks(np.asarray(inputs["L0_ffn_W1"]))
    base["b1_0"] = np.ascontiguousarray(
        np.asarray(inputs["L0_ffn_b1"]).reshape(HC, 128).T)
    base["w2_0"] = _wblocks(np.asarray(inputs["L0_ffn_W2"]))
    base["b2_0"] = _feat(inputs["L0_ffn_b2"])
    base["wip"] = _wblocks(np.ascontiguousarray(np.asarray(inputs["L1_attn_in_proj_w"]).T))
    base["cw1"] = _feat(np.asarray(inputs["L1_attn_conv_w"]).reshape(D, KER), KER)
    base["cb1"] = _feat(inputs["L1_attn_conv_b"])
    base["wxp"] = np.ascontiguousarray(
        np.asarray(inputs["L1_attn_x_proj_w"]).T.reshape(DC, 128, 33)
        .transpose(1, 0, 2).reshape(128, DC * 33))
    base["dtw"] = _feat(np.asarray(inputs["L1_attn_dt_proj_w"]).reshape(D))
    base["dtb"] = _feat(inputs["L1_attn_dt_proj_b"])
    base["alog"] = np.ascontiguousarray(np.asarray(inputs["L1_attn_A_log"]).reshape(DC, 128, NST))
    base["wop"] = _wblocks(np.ascontiguousarray(np.asarray(inputs["L1_attn_out_proj_w"]).T))
    base["w1_1"] = _wblocks(np.asarray(inputs["L1_ffn_w1"]))
    base["b1_1"] = np.ascontiguousarray(
        np.asarray(inputs["L1_ffn_b1"]).reshape(HC, 128).T)
    base["w2_1"] = _wblocks(np.asarray(inputs["L1_ffn_w2"]))
    base["b2_1"] = _feat(inputs["L1_ffn_b2"])

    consts = np.zeros((128, 4), np.float32)
    consts[:, 0] = 1.0
    consts[:, 1] = np.float32(2 ** 29)
    base["consts"] = consts
    base["bigrow"] = np.full((1, 512), np.float32(2 ** 29), np.float32)
    base["log_a_t"] = np.ascontiguousarray(
        np.asarray(inputs["L0_attn_log_a"]).reshape(DC, 128).T.astype(np.float32))

    def xslice(b, lo, hi):
        out = np.zeros((D, hi - lo + 3), np.float32)
        s = max(lo - 3, 0)
        out[:, 3 - (lo - s):] = x[b, s:hi].T
        return np.ascontiguousarray(out.reshape(DC, 128, hi - lo + 3))

    maps = []
    for c in range(8):
        b, h = c & 3, c >> 2
        m = dict(base)
        m["x_pre"] = xslice(b, 0, TP)
        m["x_own"] = m["x_pre"] if h == 0 else xslice(b, TP, 2 * TP)
        m["flag"] = np.full((128, 1), float(h), np.float32)
        maps.append(m)
    return maps


_PROG_CACHE = {}


def get_prog(dbg_names=(), dbg_phase=2):
    key = (tuple(dbg_names), dbg_phase)
    if key not in _PROG_CACHE:
        _PROG_CACHE[key] = Prog(dbg_names, dbg_phase).build()
    return _PROG_CACHE[key]


def kernel(**inputs):
    maps = stage_inputs(inputs)
    nc = get_prog()
    res = run_bass_kernel_spmd(nc, maps, core_ids=list(range(8)))
    out = np.empty((BATCH, L, D), np.float32)
    for c in range(8):
        b, h = c & 3, c >> 2
        o = res.results[c]["out"]
        out[b, h * TP:(h + 1) * TP, :] = o.reshape(D, TP).T
    return out


# version 31
# speedup vs baseline: 1.0345x; 1.0032x over previous
import numpy as np
from contextlib import ExitStack

import concourse.bass as bass
import concourse.mybir as mybir
from concourse.bass_utils import run_bass_kernel_spmd

F32 = mybir.dt.float32
BF16 = mybir.dt.bfloat16
ACT = mybir.ActivationFunctionType
ALU = mybir.AluOpType

D = 1024
DC = 8
BATCH = 4
L = 4096
TP = 2048
TC = 512
NTC = TP // TC
NST = 16
HID = 4096
HC = HID // 128
KER = 4

ENGINES = ("sync", "vector", "scalar", "tensor", "gpsimd")

DBG_TAPS = []
LIMIT_NTC = None
SKIP_P1 = False
STAGE_LIMIT = None


class Em:

    def __init__(self, nc, stk):
        self.nc = nc
        self.queues = {e: [] for e in ENGINES}
        self.sems = {}
        self.counts = {}
        for e in ENGINES:
            for kind in ("c", "d"):
                name = f"{e}_{kind}"
                self.sems[name] = stk.enter_context(nc.semaphore(name))
                self.counts[name] = 0
        for i in range(8):
            name = f"rg{i}"
            self.sems[name] = stk.enter_context(nc.semaphore(name))
            self.counts[name] = 0

    def wait(self, eng, *toks):
        best = {}
        for t in toks:
            if t is None:
                continue
            best[t[0]] = max(best.get(t[0], 0), t[1])
        for name, v in sorted(best.items()):
            if name == f"{eng}_c":
                continue
            sem = self.sems[name]
            self.queues[eng].append(lambda E, sem=sem, v=v: E.wait_ge(sem, v))

    def op(self, eng, fn, sig=False):
        if not sig:
            self.queues[eng].append(fn)
            return None
        name = f"{eng}_c"
        self.counts[name] += 1
        v = self.counts[name]
        sem = self.sems[name]

        def w(E, fn=fn, sem=sem):
            inst = fn(E)
            inst.then_inc(sem, 1)

        self.queues[eng].append(w)
        return (name, v)

    def sig_last(self, eng):
        return self.op(eng, lambda E: E.nop(), sig=True)

    def dma(self, eng, out, in_, serial=False, sem_name=None):
        name = sem_name or f"{eng}_d"
        prev = self.counts[name]
        if serial and prev:
            sem_p = self.sems[name]
            self.queues[eng].append(lambda E, s=sem_p, v=prev: E.wait_ge(s, v))
        self.counts[name] += 16
        v = self.counts[name]
        sem = self.sems[name]

        def w(E, out=out, in_=in_, sem=sem):
            E.dma_start(out=out, in_=in_).then_inc(sem, 16)

        self.queues[eng].append(w)
        return (name, v)

    def finalize(self, block):
        for e in ENGINES:
            dec = getattr(block, e)
            qq = self.queues[e]

            def body(E, qq=qq):
                for fn in qq:
                    fn(E)

            dec(body)


class Prog:
    def __init__(self, dbg_names=(), dbg_phase=2):
        self.stk = ExitStack()
        nc = bass.Bass("TRN2", target_bir_lowering=False, detect_race_conditions=False)
        self.nc = nc
        self.em = Em(nc, self.stk)
        self.pb = [self.stk.enter_context(nc.psum_tensor(f"pb{i}", [128, TC], F32))
                   for i in range(8)]
        self.pb_free = [None] * 8
        self.dbg_names = list(dbg_names)
        self.dbg_phase = dbg_phase
        self.out_toks = []
        self.chunk_guard = []

        d = {}
        dt_ = nc.dram_tensor
        d["x_pre"] = dt_("x_pre", [DC, 128, TP + 3], F32, kind="ExternalInput").ap()
        d["x_own"] = dt_("x_own", [DC, 128, TP + 3], F32, kind="ExternalInput").ap()
        d["flag"] = dt_("flag", [128, 1], F32, kind="ExternalInput").ap()
        d["wr"] = dt_("wr", [DC, 128, DC * 128], F32, kind="ExternalInput").ap()
        d["wi"] = dt_("wi", [DC, 128, DC * 128], F32, kind="ExternalInput").ap()
        d["wo0"] = dt_("wo0", [DC, 128, DC * 128], F32, kind="ExternalInput").ap()
        d["w1_0"] = dt_("w1_0", [HC, 128, DC * 128], F32, kind="ExternalInput").ap()
        d["w2_0"] = dt_("w2_0", [DC, 128, HC * 128], F32, kind="ExternalInput").ap()
        d["wip"] = dt_("wip", [2 * DC, 128, DC * 128], F32, kind="ExternalInput").ap()
        d["wop"] = dt_("wop", [DC, 128, DC * 128], F32, kind="ExternalInput").ap()
        d["w1_1"] = dt_("w1_1", [HC, 128, DC * 128], F32, kind="ExternalInput").ap()
        d["w2_1"] = dt_("w2_1", [DC, 128, HC * 128], F32, kind="ExternalInput").ap()
        d["wxp"] = dt_("wxp", [128, DC * 33], F32, kind="ExternalInput").ap()
        for nmv, cols in (("wr_b", 1), ("wi_b", 1), ("cw0", KER), ("cb0", 1),
                          ("log_a", 1), ("norm_w", 1), ("b2_0", 1),
                          ("cw1", KER), ("cb1", 1), ("dtw", 1), ("dtb", 1),
                          ("alog", NST), ("b2_1", 1)):
            d[nmv] = dt_(nmv, [DC, 128, cols], F32, kind="ExternalInput").ap()
        d["consts"] = dt_("consts", [128, 4], F32, kind="ExternalInput").ap()
        d["log_a_t"] = dt_("log_a_t", [128, DC], F32, kind="ExternalInput").ap()
        d["bigrow"] = dt_("bigrow", [1, 512], F32, kind="ExternalInput").ap()
        d["b1_0"] = dt_("b1_0", [128, HC], F32, kind="ExternalInput").ap()
        d["b1_1"] = dt_("b1_1", [128, HC], F32, kind="ExternalInput").ap()
        d["out"] = dt_("out", [DC, 128, TP], F32, kind="ExternalOutput").ap()
        d["ssm_hbm"] = dt_("ssm_hbm", [2, 33, TC], F32, kind="Internal").ap()
        self.wsplit = {}
        for wnm, mc_n, kc_n in (("wr", DC, DC), ("wi", DC, DC), ("wo0", DC, DC),
                                ("w1_0", HC, DC), ("w2_0", DC, HC),
                                ("wip", 2 * DC, DC), ("wop", DC, DC),
                                ("w1_1", HC, DC), ("w2_1", DC, HC)):
            hi = dt_(wnm + "_hi", [mc_n, 128, kc_n * 128], BF16, kind="Internal").ap()
            lo = dt_(wnm + "_lo", [mc_n, 128, kc_n * 128], BF16, kind="Internal").ap()
            self.wsplit[wnm] = (hi, lo, mc_n, kc_n)
        if self.dbg_names:
            d["dbg"] = dt_("dbg", [len(self.dbg_names), DC, 128, TP], F32,
                           kind="ExternalOutput").ap()
        self.d = d

    def sb(self, name, cols, parts=128):
        return self.stk.enter_context(self.nc.sbuf_tensor(name, [parts, cols], F32))

    def tap(self, name, tiles, tok, phase, tc):
        if name not in self.dbg_names or phase != self.dbg_phase:
            return None
        i = self.dbg_names.index(name)
        t = None
        for dc in range(DC):
            self.em.wait("gpsimd", tok)
            t = self.em.dma("gpsimd", self.d["dbg"][i, dc, :, tc * TC:(tc + 1) * TC],
                            tiles[dc][:, 0:TC])
            self.out_toks.append(t)
        return t

    def build(self):
        em = self.em
        nc = self.nc

        self.ones_col = self.sb("ones_col", 1)
        self.ones_row = self.stk.enter_context(nc.sbuf_tensor("ones_row", [1, 128], F32))
        self.row_m = self.stk.enter_context(nc.sbuf_tensor("row_m", [1, TC], F32))
        self.row_r = self.stk.enter_context(nc.sbuf_tensor("row_r", [1, TC], F32))
        self.row_t = self.stk.enter_context(nc.sbuf_tensor("row_t", [1, TC], F32))
        self.eps_ln = self.stk.enter_context(nc.sbuf_tensor("eps_ln", [1, 1], F32))
        self.eps_rms = self.stk.enter_context(nc.sbuf_tensor("eps_rms", [1, 1], F32))
        em.op("vector", lambda E: E.memset(self.ones_col[:], 1.0))
        em.op("vector", lambda E: E.memset(self.ones_row[:], 1.0))
        em.op("vector", lambda E: E.memset(self.eps_ln[:], 1e-5))
        self.init_tok = em.op("vector", lambda E: E.memset(self.eps_rms[:], 1e-6), sig=True)

        self.v = {}
        self.vtok = {}
        for nmv, cols in (("wr_b", 1), ("wi_b", 1), ("cw0", KER), ("cb0", 1),
                          ("log_a", 1), ("norm_w", 1), ("b2_0", 1),
                          ("cw1", KER), ("cb1", 1), ("dtw", 1), ("dtb", 1),
                          ("alog", NST), ("b2_1", 1)):
            t = self.sb("v_" + nmv, DC * cols)
            tk = None
            for dc in range(DC):
                tk = em.dma("sync", t[:, dc * cols:(dc + 1) * cols], self.d[nmv][dc])
            self.v[nmv] = t
            self.vtok[nmv] = tk
        for nmv in ("b1_0", "b1_1"):
            t = self.sb("v_" + nmv, HC)
            self.v[nmv] = t
            self.vtok[nmv] = em.dma("sync", t[:], self.d[nmv])
        self.flag = self.sb("flag_sb", 1)
        self.vtok["flag"] = em.dma("sync", self.flag[:], self.d["flag"])
        self.wxp = self.sb("wxp_sb", DC * 33)
        self.vtok["wxp"] = em.dma("sync", self.wxp[:], self.d["wxp"])
        self.vec_tok = self.vtok["flag"]

        self.Aneg = self.sb("Aneg", DC * NST)
        em.wait("scalar", self.vtok["alog"])
        self.A_tok0 = em.op("scalar", lambda E: E.activation(self.Aneg[:], self.v["alog"][:],
                                                             ACT.Exp, bias=0.0, scale=1.0),
                            sig=True)
        em.wait("vector", self.A_tok0)
        self.A_tok = em.op("vector", lambda E: E.tensor_scalar_mul(self.Aneg[:], self.Aneg[:],
                                                                   -1.0), sig=True)

        self.carry0 = self.sb("carry0", DC)
        self.carry1 = self.sb("carry1", DC * NST)
        self.halo1 = self.sb("halo1", DC * 3)
        self.st0 = self.sb("st0", DC)
        self.st1 = self.sb("st1", DC * NST)
        self.hal = self.sb("hal", DC * 3)

        self.NSLOTB = 80
        self.ringb = self.stk.enter_context(
            nc.sbuf_tensor("wringb", [128, self.NSLOTB * 128], BF16))
        self.ringb_free = [None] * self.NSLOTB
        self.ringb_head = 0

        self.bcB = self.sb("bcB", TC)
        self.bcC = self.sb("bcC", TC)
        self.bcS = self.sb("bcS", TC)

        self.ssm_slot = 0

        W512 = 512
        self.c_a = self.sb("c_a", DC)
        setup_scope = ExitStack()
        ssb = lambda nm: setup_scope.enter_context(
            self.nc.sbuf_tensor(nm, [128, W512], F32))
        self.gpad = ssb("gpad")
        self.gpadp = ssb("gpadp")
        self.gate = ssb("gate")
        self.capad = ssb("capad")
        self.cat1 = ssb("cat1")
        self.cat2 = ssb("cat2")
        big_bc = bass.AP(self.d["bigrow"].tensor, 0, [[0, 128], [1, W512]])
        gtok = em.dma("sync", self.gpad[:], big_bc)
        em.wait("vector", gtok, self.init_tok)
        em.op("vector", lambda E: E.tensor_scalar_add(self.gpadp[:], self.gpad[:], 1.0))
        self.gate_tok = em.op("vector", lambda E: E.tensor_sub(self.gate[:], self.gpadp[:],
                                                               self.gpad[:]), sig=True)

        def round32(v):
            em.op("vector", lambda E: E.tensor_tensor(self.cat1[:], v[:], self.gpadp[:],
                                                      ALU.mult))
            em.op("vector", lambda E: E.tensor_tensor(self.cat2[:], self.cat1[:], v[:],
                                                      ALU.subtract))
            em.op("vector", lambda E: E.tensor_tensor(self.cat1[:], self.cat1[:],
                                                      self.cat2[:], ALU.subtract))
            em.op("vector", lambda E: E.tensor_tensor(self.cat2[:], self.cat1[:], v[:],
                                                      ALU.subtract))
            return em.op("vector", lambda E: E.scalar_tensor_tensor(
                v[:], self.cat2[:], self.gate[:, 0:1], v[:], ALU.mult, ALU.add), sig=True)

        zt = em.op("vector", lambda E: E.memset(self.capad[:], 0.0), sig=True)
        em.wait("sync", zt)
        la_tok = em.dma("sync", self.capad[:, 0:DC], self.d["log_a_t"])
        em.wait("scalar", la_tok)
        e_tok = em.op("scalar", lambda E: E.activation(self.capad[:], self.capad[:],
                                                       ACT.Exp, bias=0.0, scale=-1.0),
                      sig=True)
        em.wait("vector", e_tok, self.gate_tok)
        round32(self.capad)
        em.op("vector", lambda E: E.tensor_scalar_add(self.capad[:], self.capad[:], 1.0))
        round32(self.capad)
        em.op("vector", lambda E: E.reciprocal(self.capad[:], self.capad[:]))
        ab_tok = round32(self.capad)
        em.wait("scalar", ab_tok)
        em.op("scalar", lambda E: E.activation(self.capad[:], self.capad[:],
                                               ACT.Ln, bias=0.0, scale=1.0))
        ca8_tok = em.op("scalar", lambda E: E.mul(self.capad[:], self.capad[:], 8.0),
                        sig=True)
        em.wait("vector", ca8_tok)
        self.ca_tok = em.op("vector", lambda E: E.tensor_copy(self.c_a[:], self.capad[:, 0:DC]),
                            sig=True)
        self.setup_scope = setup_scope


        self.prep_weights()
        self.chunk_guard = [self.prep_done]

        em.wait("sync", self.ca_tok)
        self.setup_scope.close()
        if not SKIP_P1:
            self.phase(1)
        else:
            em.op("vector", lambda E: E.memset(self.carry0[:], 0.0))
            em.op("vector", lambda E: E.memset(self.carry1[:], 0.0))
            self.ph1_tok = em.op("vector", lambda E: E.memset(self.halo1[:], 0.0), sig=True)
            self.scan0_tok = self.ph1_tok
            self.scan1_tok = self.ph1_tok
        self.phase(2)

        em.wait("gpsimd", self.out_toks[-1])
        em.op("gpsimd", lambda E: E.nop())

        with nc.Block() as block:
            em.finalize(block)
        self.stk.close()
        return nc


    def prep_weights(self):
        em = self.em
        nc = self.nc
        self.wready = {}
        self._wsync_done = set()
        self._ring_lane = 0
        order = ["wr", "wi", "wo0", "w1_0", "w2_0", "wip", "wop", "w1_1", "w2_1"]
        with ExitStack() as ps:
            buf = [ps.enter_context(nc.sbuf_tensor(f"prep{i}", [128, HC * 128], F32))
                   for i in range(2)]
            bhi = [ps.enter_context(nc.sbuf_tensor(f"prephi{i}", [128, HC * 128], BF16))
                   for i in range(2)]
            blo = [ps.enter_context(nc.sbuf_tensor(f"preplo{i}", [128, HC * 128], BF16))
                   for i in range(2)]
            free_tok = [None, None]
            i = 0
            for wnm in order:
                hi_t, lo_t, mc_n, kc_n = self.wsplit[wnm]
                cols = kc_n * 128
                toks = []
                for mc in range(mc_n):
                    b = i % 2
                    i += 1
                    em.wait("sync", free_tok[b])
                    ld = em.dma("sync", buf[b][:, 0:cols], self.d[wnm][mc],
                                serial=True)
                    em.wait("scalar", ld)
                    h_tok = em.op("scalar", lambda E, o=bhi[b][:, 0:cols],
                                  s=buf[b][:, 0:cols]:
                                  E.activation(o, s, ACT.Identity, bias=0.0, scale=1.0),
                                  sig=True)
                    em.wait("vector", h_tok)
                    l_tok = em.op("vector", lambda E, o=blo[b][:, 0:cols],
                                  s=buf[b][:, 0:cols], h2=bhi[b][:, 0:cols]:
                                  E.tensor_tensor(o, s, h2, ALU.subtract), sig=True)
                    em.wait("sync", l_tok)
                    t1 = em.dma("sync", hi_t[mc], bhi[b][:, 0:cols], serial=True)
                    t2 = em.dma("sync", lo_t[mc], blo[b][:, 0:cols], serial=True)
                    free_tok[b] = t2
                    toks.append(t2)
                self.wready[wnm] = toks[-1]
            self.prep_done = toks[-1]

    def wload(self, hbm_block_ap, n_tiles):
        runs = []
        need = n_tiles
        while need > 0:
            take = min(need, 16)
            if self.ring_head + take > self.NSLOT:
                self.ring_head = 0
            s0 = self.ring_head
            self.ring_head += take
            runs.append((s0, take))
            need -= take
        slots = []
        toks = []
        col0 = 0
        for s0, take in runs:
            for s in range(s0, s0 + take):
                self.em.wait("sync", self.ring_free[s])
            t = self.em.dma("sync", self.ring[:, s0 * 128:(s0 + take) * 128],
                            hbm_block_ap[:, col0 * 128:(col0 + take) * 128])
            toks.append(t)
            slots.extend(range(s0, s0 + take))
            col0 += take
        return slots, toks

    def mm_block(self, w_hbm_ap, n_k, rhs_tiles, rhs_tok, bank, extra=()):
        em = self.em
        slots, dtoks = self.wload(w_hbm_ap, n_k)
        pb = self.pb[bank]
        em.wait("tensor", rhs_tok, self.pb_free[bank], *dtoks, *extra)
        mm_tok = None
        for j, kc in enumerate(range(n_k)):
            s = slots[j]
            w_ap = self.ring[:, s * 128:(s + 1) * 128]
            mm_tok = em.op("tensor", lambda E, pb=pb, w=w_ap, r=rhs_tiles[kc],
                           a=(kc == 0), o=(kc == n_k - 1):
                           E.matmul(pb[:], w, r[:, 0:TC], start=a, stop=o),
                           sig=(kc == n_k - 1))
        for s in slots:
            self.ring_free[s] = mm_tok
        return mm_tok


    def wloadb(self, hbm_block_ap, n_tiles, lane=None):
        runs = []
        need = n_tiles
        while need > 0:
            take = min(need, 16)
            if self.ringb_head + take > self.NSLOTB:
                self.ringb_head = 0
            s0 = self.ringb_head
            self.ringb_head += take
            runs.append((s0, take))
            need -= take
        slots = []
        toks = []
        col0 = 0
        for s0, take in runs:
            for s in range(s0, s0 + take):
                self.em.wait("sync", self.ringb_free[s])
            t = self.em.dma("sync", self.ringb[:, s0 * 128:(s0 + take) * 128],
                            hbm_block_ap[:, col0 * 128:(col0 + take) * 128],
                            sem_name=lane)
            toks.append(t)
            slots.extend(range(s0, s0 + take))
            col0 += take
        return slots, toks

    def mm_block3(self, wnm, mc, n_k, rhs_hi, rhs_lo, rhs_tok, bank, extra=()):
        em = self.em
        hi_t, lo_t, _, _ = self.wsplit[wnm]
        pb = self.pb[bank]
        if wnm not in self._wsync_done:
            em.wait("sync", self.wready.get(wnm))
            self._wsync_done.add(wnm)
        rt_list = rhs_tok if isinstance(rhs_tok, list) else None
        em.wait("tensor", None if rt_list else rhs_tok, self.pb_free[bank],
                self.wready.get(wnm), *extra)
        mm_tok = None
        n_terms = 3 * n_k
        j = 0
        for k0 in range(0, n_k, 8):
            kn = min(8, n_k - k0)
            lane = f"rg{self._ring_lane}"
            self._ring_lane = (self._ring_lane + 1) % 8
            s_hi, d1 = self.wloadb(hi_t[mc][:, k0 * 128:(k0 + kn) * 128], kn,
                                   lane=lane)
            s_lo, d2 = self.wloadb(lo_t[mc][:, k0 * 128:(k0 + kn) * 128], kn,
                                   lane=lane)
            rt = ([rt_list[min(k0 + kn - 1, len(rt_list) - 1)]]
                  if rt_list else [])
            em.wait("tensor", *d1, *d2, *rt)
            for kc in range(kn):
                wh = self.ringb[:, s_hi[kc] * 128:(s_hi[kc] + 1) * 128]
                wl = self.ringb[:, s_lo[kc] * 128:(s_lo[kc] + 1) * 128]
                last_sub = (kc == kn - 1)
                for ti, (w_ap, r_ap) in enumerate(((wh, rhs_hi[k0 + kc]),
                                                   (wh, rhs_lo[k0 + kc]),
                                                   (wl, rhs_hi[k0 + kc]))):
                    mm_tok = em.op("tensor", lambda E, pb=pb, w=w_ap, r=r_ap,
                                   a=(j == 0), o=(j == n_terms - 1):
                                   E.matmul(pb[:], w, r[:, 0:TC], start=a, stop=o),
                                   sig=(last_sub and ti == 2))
                    j += 1
            for s in s_hi:
                self.ringb_free[s] = mm_tok
            for s in s_lo:
                self.ringb_free[s] = mm_tok
        return mm_tok

    def split(self, tset_fn, nm, tiles, tok, n=DC):
        em = self.em
        th = tset_fn(f"{nm}_hi", TC, BF16)
        tl = tset_fn(f"{nm}_lo", TC, BF16)
        toks = []
        for dc in range(n):
            em.wait("scalar", tok)
            hcast = em.op("scalar", lambda E, o=th[dc], s=tiles[dc]:
                          E.activation(o[:], s[:, 0:TC], ACT.Identity,
                                       bias=0.0, scale=1.0), sig=True)
            em.wait("vector", hcast)
            toks.append(em.op("vector",
                              lambda E, o=tl[dc], s=tiles[dc], h2=th[dc]:
                              E.tensor_tensor(o[:], s[:, 0:TC], h2[:],
                                              ALU.subtract), sig=True))
        return th, tl, toks

    def keeper(self, tok):
        em = self.em
        em.wait("tensor", tok)
        em.op("tensor", lambda E: E.matmul(self.pb[4][64:65, 0:1],
                                           self.ones_row[0:1, 0:1],
                                           self.ones_col[0:1, 0:1],
                                           start=True, stop=True))

    def bcast_row_psum(self, row_ap, bank, tok):
        em = self.em
        pb = self.pb[bank]
        em.wait("tensor", tok, self.pb_free[bank])
        return em.op("tensor", lambda E: E.matmul(pb[:], self.ones_row[:], row_ap,
                                                  start=True, stop=True), sig=True)

    def ln_stats(self, y, y_tok, sq, bank_s, bank_q, eps_ap, want_mean):
        em = self.em
        pbs, pbq = self.pb[bank_s], self.pb[bank_q]
        sq_tok = None
        for dc in range(DC):
            em.wait("scalar", y_tok)
            sq_tok = em.op("scalar", lambda E, o=sq[dc], i=y[dc]:
                           E.activation(o[:, 0:TC], i[:, 0:TC], ACT.Square,
                                        bias=0.0, scale=1.0), sig=(dc == DC - 1))
        em.wait("tensor", y_tok, sq_tok, self.pb_free[bank_s], self.pb_free[bank_q])
        if want_mean:
            for dc in range(DC):
                em.op("tensor", lambda E, r=y[dc], a=(dc == 0), o=(dc == DC - 1):
                      E.matmul(pbs[0:1, :], self.ones_col[:], r[:, 0:TC],
                               start=a, stop=o))
        st_tok = None
        for dc in range(DC):
            st_tok = em.op("tensor", lambda E, r=sq[dc], a=(dc == 0), o=(dc == DC - 1):
                           E.matmul(pbq[0:1, :], self.ones_col[:], r[:, 0:TC],
                                    start=a, stop=o), sig=(dc == DC - 1))
        em.wait("vector", st_tok)
        if want_mean:
            em.op("vector", lambda E: E.tensor_scalar_mul(self.row_m[:], pbs[0:1, :], 1.0 / D))
            em.op("vector", lambda E: E.tensor_scalar_mul(self.row_r[:], pbq[0:1, :], 1.0 / D))
            em.op("vector", lambda E: E.tensor_tensor(self.row_t[:], self.row_m[:],
                                                      self.row_m[:], ALU.mult))
            v_tok = em.op("vector", lambda E: E.tensor_sub(self.row_r[:], self.row_r[:],
                                                           self.row_t[:]), sig=True)
        else:
            v_tok = em.op("vector", lambda E: E.tensor_scalar_mul(self.row_r[:], pbq[0:1, :],
                                                                  1.0 / D), sig=True)
        em.wait("scalar", v_tok)
        s_tok = em.op("scalar", lambda E: E.activation(self.row_r[:], self.row_r[:],
                                                       ACT.Sqrt, bias=eps_ap, scale=1.0),
                      sig=True)
        em.wait("vector", s_tok)
        if want_mean:
            em.op("vector", lambda E: E.reciprocal(self.row_r[:], self.row_r[:]))
            out_tok = em.op("vector", lambda E: E.tensor_tensor(self.row_t[:], self.row_m[:],
                                                                self.row_r[:], ALU.mult),
                            sig=True)
        else:
            out_tok = em.op("vector", lambda E: E.reciprocal(self.row_r[:], self.row_r[:]),
                            sig=True)
        self.pb_free[bank_s] = out_tok
        self.pb_free[bank_q] = out_tok
        return out_tok

    def layernorm(self, y, y_tok, out, sq):
        em = self.em
        st = self.ln_stats(y, y_tok, sq, 4, 5, self.eps_ln[0:1, 0:1], True)
        b_r = self.bcast_row_psum(self.row_r[:], 6, st)
        b_mr = self.bcast_row_psum(self.row_t[:], 7, st)
        pbr, pbm = self.pb[6], self.pb[7]
        o_tok = None
        for dc in range(DC):
            em.wait("vector", b_r, b_mr)
            em.op("vector", lambda E, o=out[dc], i=y[dc]:
                  E.tensor_tensor(o[:, 0:TC], i[:, 0:TC], pbr[:], ALU.mult))
            o_tok = em.op("vector", lambda E, o=out[dc]:
                          E.tensor_tensor(o[:, 0:TC], o[:, 0:TC], pbm[:], ALU.subtract),
                          sig=(dc == DC - 1))
        self.pb_free[6] = o_tok
        self.pb_free[7] = o_tok
        return o_tok

    def phase(self, phase):
        em = self.em
        is_p1 = (phase == 1)
        xin_hbm = self.d["x_pre"] if is_p1 else self.d["x_own"]

        if is_p1:
            em.wait("vector", self.init_tok)
            em.op("vector", lambda E: E.memset(self.st0[:], 0.0))
            em.op("vector", lambda E: E.memset(self.st1[:], 0.0))
            self.state_tok = em.op("vector", lambda E: E.memset(self.hal[:], 0.0), sig=True)
        else:
            em.wait("vector", self.ph1_tok, self.vtok["flag"])
            em.op("vector", lambda E: E.tensor_scalar(self.st0[:], self.carry0[:],
                                                      self.flag[:, 0:1], None, ALU.mult))
            em.op("vector", lambda E: E.tensor_scalar(self.st1[:], self.carry1[:],
                                                      self.flag[:, 0:1], None, ALU.mult))
            self.state_tok = em.op("vector", lambda E:
                                   E.tensor_scalar(self.hal[:], self.halo1[:],
                                                   self.flag[:, 0:1], None, ALU.mult),
                                   sig=True)

        for tc in range(LIMIT_NTC or NTC):
            self.chunk(phase, tc, xin_hbm)

        if is_p1:
            if not hasattr(self, 'scan1_tok'):
                self.scan1_tok = self.scan0_tok if hasattr(self, 'scan0_tok') else None
            if not hasattr(self, 'scan0_tok'):
                self.scan0_tok = None
            em.wait("vector", self.scan0_tok, self.scan1_tok)
            em.op("vector", lambda E: E.tensor_copy(self.carry0[:], self.st0[:]))
            em.op("vector", lambda E: E.tensor_copy(self.carry1[:], self.st1[:]))
            self.ph1_tok = em.op("vector", lambda E: E.tensor_copy(self.halo1[:], self.hal[:]),
                                 sig=True)

    def chunk(self, phase, tc, xin_hbm):
        em = self.em
        nc = self.nc
        is_p1 = (phase == 1)
        nm = f"p{phase}t{tc}"

        with ExitStack() as sc:
            def tset(name, cols=TC, dtype=F32):
                t = sc.enter_context(nc.sbuf_tensor(f"{nm}_{name}", [128, DC * cols], dtype))
                return [t[:, i * cols:(i + 1) * cols] for i in range(DC)]

            em.wait("sync", *self.chunk_guard)
            em.wait("gpsimd", *self.chunk_guard)
            rs = tset("rs")

            self.l0_attn(nm, sc, tset, xin_hbm, rs, phase, tc)
            if STAGE_LIMIT in (None, 'ffn', 'l1'):
                self.ffn(nm + "f0", rs, self.rs_tok, "w1_0", "b1_0", "w2_0", "b2_0", phase, tc)
            if STAGE_LIMIT in (None, 'l1'):
                self.l1_attn(nm, sc, tset, rs, self.rs_tok, phase, tc)
            if not is_p1 and STAGE_LIMIT is None:
                self.ffn(nm + "f1", rs, self.rs_tok, "w1_1", "b1_1", "w2_1", "b2_1", phase, tc)
                self.tap("out", rs, self.rs_tok, phase, tc)
                for dc in range(DC):
                    em.wait("gpsimd", self.rs_tok)
                    t = em.dma("gpsimd", self.d["out"][dc, :, tc * TC:(tc + 1) * TC],
                               rs[dc][:, 0:TC])
                    self.out_toks.append(t)
            elif not is_p1:
                tiles = getattr(self, 'dump_tiles', None) or rs
                for dc in range(DC):
                    em.wait("gpsimd", self.rs_tok)
                    t = em.dma("gpsimd", self.d["out"][dc, :, tc * TC:(tc + 1) * TC],
                               tiles[dc][:, 0:TC])
                    self.out_toks.append(t)
            ks = getattr(self, 'keep_scope', None)
            if ks is not None:
                ks.close()
                self.keep_scope = None
                self.dump_tiles = None
            self.chunk_guard = [self.em.tokq("vector"), self.em.tokq("scalar"),
                                self.em.tokq("tensor"), self.em.tokq("gpsimd")]

    def l0_attn(self, nm, sc_outer, tset_outer, xin_hbm, rs, phase, tc):
        em = self.em
        nc = self.nc
        v = self.v
        lsc = ExitStack()
        def tset(name, cols=TC, dtype=F32):
            t = lsc.enter_context(nc.sbuf_tensor(f"{nm}0_{name}", [128, DC * cols], dtype))
            return [t[:, i * cols:(i + 1) * cols] for i in range(DC)]
        x = tset("x", TC + 3)
        x_tok = None
        for dc in range(DC):
            x_tok = em.dma("sync", x[dc], xin_hbm[dc, :, tc * TC: tc * TC + TC + 3])
        xc = tset("xc")
        c_tok = None
        for dc in range(DC):
            em.wait("vector", x_tok, self.vtok["cb0"])
            em.op("vector", lambda E, o=xc[dc], i=x[dc], dc=dc:
                  E.tensor_scalar(o[:], i[:, 3:3 + TC],
                                  v["cw0"][:, dc * KER + KER - 1:dc * KER + KER],
                                  v["cb0"][:, dc:dc + 1], ALU.mult, ALU.add))
            for k in range(KER - 1):
                c_tok = em.op("vector", lambda E, o=xc[dc], i=x[dc], dc=dc, k=k:
                              E.scalar_tensor_tensor(o[:], i[:, k:k + TC],
                                                     v["cw0"][:, dc * KER + k:dc * KER + k + 1],
                                                     o[:], ALU.mult, ALU.add),
                              sig=(dc == DC - 1 and k == KER - 2))
        self.tap("xc0", xc, c_tok, phase, tc)
        if STAGE_LIMIT == 'conv':
            self.rs_tok = c_tok
            self.dump_tiles = xc
            self.scan0_tok = c_tok
            self.keep_scope = lsc
            return
        xch, xcl, xcsp_tok = self.split(tset, "xcs0", xc, c_tok)
        r = tset("r")
        ii = tset("i")
        for mc in range(DC):
            bank = mc % 2
            mm = self.mm_block3("wr", mc, DC, xch, xcl, xcsp_tok, bank)
            em.wait("scalar", mm, self.vtok["wr_b"])
            e1 = em.op("scalar", lambda E, o=r[mc], pb=self.pb[bank], mc=mc:
                       E.activation(o[:], pb[:], ACT.Sigmoid,
                                    bias=v["wr_b"][:, mc:mc + 1], scale=1.0), sig=True)
            self.pb_free[bank] = e1
            bank = 2 + mc % 2
            mm = self.mm_block3("wi", mc, DC, xch, xcl, xcsp_tok, bank)
            em.wait("scalar", mm, self.vtok["wi_b"])
            e2 = em.op("scalar", lambda E, o=ii[mc], pb=self.pb[bank], mc=mc:
                       E.activation(o[:], pb[:], ACT.Sigmoid,
                                    bias=v["wi_b"][:, mc:mc + 1], scale=1.0), sig=True)
            self.pb_free[bank] = e2
        ri_tok = e2
        rg_t = self.tap("rg", r, ri_tok, phase, tc)
        ig_t = self.tap("ig", ii, ri_tok, phase, tc)
        if STAGE_LIMIT == 'gates':
            self.rs_tok = ri_tok
            self.dump_tiles = r
            self.scan0_tok = ri_tok
            self.keep_scope = lsc
            return
        a = r
        a_tok = None
        for dc in range(DC):
            em.wait("scalar", self.ca_tok, rg_t)
            a_tok = em.op("scalar", lambda E, o=a[dc], dc=dc:
                          E.activation(o[:], o[:], ACT.Exp, bias=0.0,
                                       scale=self.c_a[:, dc:dc + 1]), sig=(dc == DC - 1))
        b = tset("b")
        b_tok = None
        for dc in range(DC):
            em.wait("vector", a_tok)
            em.op("vector", lambda E, t=b[dc], i=a[dc]:
                  E.tensor_tensor(t[:], i[:], i[:], ALU.mult))
            em.op("vector", lambda E, t=b[dc]:
                  E.tensor_scalar(t[:], t[:], -1.0, 1.0, ALU.mult, ALU.add))
            tv = em.op("vector", lambda E, t=b[dc]:
                       E.tensor_scalar_max(t[:], t[:], 1e-6), sig=True)
            em.wait("scalar", tv)
            s_tok = em.op("scalar", lambda E, t=b[dc]:
                          E.activation(t[:], t[:], ACT.Sqrt, bias=0.0, scale=1.0), sig=True)
            em.wait("vector", s_tok)
            em.op("vector", lambda E, t=b[dc], i=ii[dc]:
                  E.tensor_tensor(t[:], t[:], i[:], ALU.mult))
            b_tok = em.op("vector", lambda E, t=b[dc], c=xc[dc]:
                          E.tensor_tensor(t[:], t[:], c[:], ALU.mult),
                          sig=(dc == DC - 1))
        aa_t = self.tap("aa", a, a_tok, phase, tc)
        bb_t = self.tap("bb", b, b_tok, phase, tc)
        h = ii
        sc_tok = None
        em.wait("vector", b_tok, self.state_tok, ig_t, aa_t, bb_t)
        for dc in range(DC):
            t = em.op("vector", lambda E, o=h[dc], A=a[dc], Bb=b[dc], dc=dc:
                      E.tensor_tensor_scan(o[:], A[:], Bb[:], self.st0[:, dc:dc + 1],
                                           ALU.mult, ALU.add), sig=True)
            self.keeper(t)
        for dc in range(DC):
            sc_tok = em.op("vector", lambda E, o=h[dc], dc=dc:
                           E.tensor_copy(self.st0[:, dc:dc + 1], o[:, TC - 1:TC]),
                           sig=(dc == DC - 1))
        self.scan0_tok = sc_tok
        self.tap("h0", h, sc_tok, phase, tc)
        if STAGE_LIMIT == 'scan':
            self.rs_tok = sc_tok
            self.dump_tiles = h
            self.keep_scope = lsc
            return
        hsp_h, hsp_l, hsp_tok = self.split(tset, "hsp", h, sc_tok)
        o = a
        sq = b
        ev_tok = None
        for mc in range(DC):
            bank = mc % 2
            mm = self.mm_block3("wo0", mc, DC, hsp_h, hsp_l, hsp_tok, bank)
            em.wait("scalar", mm)
            em.op("scalar", lambda E, o2=o[mc], pb=self.pb[bank]:
                  E.activation(o2[:], pb[:], ACT.Identity, bias=0.0, scale=1.0))
            ev_tok = em.op("scalar", lambda E, s=sq[mc], pb=self.pb[bank]:
                           E.activation(s[:], pb[:], ACT.Square, bias=0.0, scale=1.0),
                           sig=True)
            self.pb_free[bank] = ev_tok
        st = self.ln_stats(o, ev_tok, sq, 4, 5, self.eps_rms[0:1, 0:1], False)
        bc = self.bcast_row_psum(self.row_r[:], 6, st)
        y1 = h
        y_tok = None
        for dc in range(DC):
            em.wait("vector", bc, self.vtok["norm_w"])
            em.op("vector", lambda E, t=o[dc]:
                  E.tensor_tensor(t[:], t[:], self.pb[6][:], ALU.mult))
            y_tok = em.op("vector", lambda E, y=y1[dc], t=o[dc], xx=x[dc], dc=dc:
                          E.scalar_tensor_tensor(y[:], t[:], v["norm_w"][:, dc:dc + 1],
                                                 xx[:, 3:3 + TC], ALU.mult, ALU.add),
                          sig=(dc == DC - 1))
        self.pb_free[6] = y_tok
        self.tap("y1", y1, y_tok, phase, tc)
        self.rs_tok = self.layernorm(y1, y_tok, rs, sq)
        self.tap("ln1", rs, self.rs_tok, phase, tc)
        lsc.close()

    def ffn(self, nm, rs, rs_tok, w1, b1, w2, b2, phase, tc):
        em = self.em
        nc = self.nc
        v = self.v
        with ExitStack() as fsc:
            def ftset(name, cols=TC, dtype=F32, n=DC):
                t = fsc.enter_context(nc.sbuf_tensor(f"{nm}_{name}", [128, n * cols], dtype))
                return [t[:, i * cols:(i + 1) * cols] for i in range(n)]
            rsh, rsl, rsp_tok = self.split(ftset, "rsp", rs, rs_tok)
            hid_h = ftset("hidh", TC, BF16, HC)
            hid_l = ftset("hidl", TC, BF16, HC)
            y2 = ftset("y2")
            sqs = ftset("sq")
            hft = ftset("hft", TC, F32, 2)
            hft_guard = [None, None]
            h_tok = None
            for hc in range(HC):
                bank = hc % 4
                mm = self.mm_block3(w1, hc, DC, rsh, rsl, rsp_tok, bank)
                em.wait("scalar", mm, self.vtok[b1], hft_guard[hc % 2])
                hf_tok = em.op("scalar", lambda E, o=hft[hc % 2], pb=self.pb[bank], hc=hc:
                               E.activation(o[:], pb[:], ACT.Relu,
                                            bias=v[b1][:, hc:hc + 1], scale=1.0), sig=True)
                self.pb_free[bank] = hf_tok
                hh_tok = em.op("scalar", lambda E, o=hid_h[hc], s=hft[hc % 2]:
                               E.activation(o[:], s[:], ACT.Identity,
                                            bias=0.0, scale=1.0), sig=True)
                em.wait("vector", hh_tok)
                h_tok = em.op("vector", lambda E, o=hid_l[hc], s=hft[hc % 2], h2=hid_h[hc]:
                              E.tensor_tensor(o[:], s[:], h2[:], ALU.subtract),
                              sig=True)
                hft_guard[hc % 2] = h_tok
            e_tok = None
            for oc in range(DC):
                bank = 2 + oc % 2
                mm = self.mm_block3(w2, oc, HC, hid_h, hid_l, h_tok, bank)
                em.wait("scalar", mm, self.vtok[b2])
                ev = em.op("scalar", lambda E, o=y2[oc], pb=self.pb[bank], oc=oc:
                           E.activation(o[:], pb[:], ACT.Identity,
                                        bias=v[b2][:, oc:oc + 1], scale=1.0), sig=True)
                self.pb_free[bank] = ev
                em.wait("vector", ev)
                e_tok = em.op("vector", lambda E, o=y2[oc], r=rs[oc]:
                              E.tensor_add(o[:], o[:], r[:]), sig=(oc == DC - 1))
            self.rs_tok = self.layernorm(y2, e_tok, rs, sqs)
            self.tap("ffn" + nm[-2:], rs, self.rs_tok, phase, tc)

    def l1_attn(self, nm, sc_outer, tset_outer, rs, rs_tok, phase, tc):
        em = self.em
        nc = self.nc
        v = self.v
        is_p1 = (phase == 1)
        with ExitStack() as lsc:
            def tset(name, cols=TC, dtype=F32):
                t = lsc.enter_context(nc.sbuf_tensor(f"{nm}L_{name}", [128, DC * cols], dtype))
                return [t[:, i * cols:(i + 1) * cols] for i in range(DC)]

            xin = tset("xin", TC + 3)
            z = tset("z") if not is_p1 else None
            with ExitStack() as rsc:
                def rtset(name, cols=TC, dtype=F32):
                    t = rsc.enter_context(
                        nc.sbuf_tensor(f"{nm}Lr_{name}", [128, DC * cols], dtype))
                    return [t[:, i * cols:(i + 1) * cols] for i in range(DC)]
                rsh, rsl, rsp_tok = self.split(rtset, "rsp1", rs, rs_tok)
                ip_tok = None
                for mc in range(DC):
                    bank = mc % 2
                    mm = self.mm_block3("wip", mc, DC, rsh, rsl, rsp_tok, bank)
                    em.wait("scalar", mm)
                    ip_tok = em.op("scalar", lambda E, o=xin[mc], pb=self.pb[bank]:
                                   E.activation(o[:, 3:3 + TC], pb[:], ACT.Identity,
                                                bias=0.0, scale=1.0), sig=True)
                    self.pb_free[bank] = ip_tok
                if not is_p1:
                    for mc in range(DC):
                        bank = 2 + mc % 2
                        mm = self.mm_block3("wip", DC + mc, DC, rsh, rsl, rsp_tok, bank)
                        em.wait("scalar", mm)
                        zt = em.op("scalar", lambda E, o=z[mc], pb=self.pb[bank]:
                                   E.activation(o[:], pb[:], ACT.Identity,
                                                bias=0.0, scale=1.0), sig=True)
                        self.pb_free[bank] = zt
                self.rsp1_guard = ip_tok if is_p1 else zt
            hl_tok = None
            for dc in range(DC):
                em.wait("gpsimd", ip_tok, self.state_tok if tc == 0 else self.halo_tok)
                hl_tok = em.op("gpsimd", lambda E, o=xin[dc], dc=dc:
                               E.tensor_copy(o[:, 0:3], self.hal[:, dc * 3:dc * 3 + 3]),
                               sig=(dc == DC - 1))
            nh_tok = None
            for dc in range(DC):
                em.wait("vector", ip_tok, hl_tok)
                nh_tok = em.op("vector", lambda E, i=xin[dc], dc=dc:
                               E.tensor_copy(self.hal[:, dc * 3:dc * 3 + 3],
                                             i[:, TC:TC + 3]),
                               sig=(dc == DC - 1))
            self.halo_tok = nh_tok
            xcs = tset("xcs")
            cs_tok = None
            for dc in range(DC):
                em.wait("vector", hl_tok, self.vtok["cb1"], self.rsp1_guard)
                em.op("vector", lambda E, o=xcs[dc], i=xin[dc], dc=dc:
                      E.tensor_scalar(o[:], i[:, 3:3 + TC],
                                      v["cw1"][:, dc * KER + KER - 1:dc * KER + KER],
                                      v["cb1"][:, dc:dc + 1], ALU.mult, ALU.add))
                for k in range(KER - 1):
                    cs_tok = em.op("vector", lambda E, o=xcs[dc], i=xin[dc], dc=dc, k=k:
                                   E.scalar_tensor_tensor(o[:], i[:, k:k + TC],
                                                          v["cw1"][:, dc * KER + k:dc * KER + k + 1],
                                                          o[:], ALU.mult, ALU.add),
                                   sig=(dc == DC - 1 and k == KER - 2))
            sl_tok = None
            for dc in range(DC):
                em.wait("scalar", cs_tok)
                sl_tok = em.op("scalar", lambda E, o=xcs[dc]:
                               E.activation(o[:], o[:], ACT.Silu, bias=0.0, scale=1.0),
                               sig=(dc == DC - 1))
            self.tap("xcs", xcs, sl_tok, phase, tc)
            pb4 = self.pb[4]
            em.wait("tensor", sl_tok, self.pb_free[4], self.vtok["wxp"])
            xp_tok = None
            for kc in range(DC):
                xp_tok = em.op("tensor", lambda E, kc=kc:
                               E.matmul(pb4[0:33, :], self.wxp[:, kc * 33:(kc + 1) * 33],
                                        xcs[kc][:, 0:TC], start=(kc == 0),
                                        stop=(kc == DC - 1)), sig=(kc == DC - 1))
            self.xp_done = xp_tok
            ssm = lsc.enter_context(nc.sbuf_tensor(f"{nm}L_ssm", [33, TC], F32))
            em.wait("vector", xp_tok)
            ssm_tok = em.op("vector", lambda E: E.tensor_copy(ssm[:], pb4[0:33, :]), sig=True)
            self.pb_free[4] = ssm_tok
            slot = self.ssm_slot
            self.ssm_slot ^= 1
            em.wait("sync", ssm_tok, getattr(self, f"_ssm_guard{slot}", None))
            ssm_hbm_tok = em.dma("sync", self.d["ssm_hbm"][slot], ssm[:])
            em.wait("sync", ssm_hbm_tok)
            row = self.d["ssm_hbm"].tensor
            def hrow(r):
                off = (slot * 33 + r) * TC
                return bass.AP(row, off, [[0, 128], [1, TC]])
            s_tok = em.dma("sync", self.bcS[:], hrow(32))
            dt = tset("dt")
            dts_t = lsc.enter_context(nc.sbuf_tensor(f"{nm}L_dts", [128, 2 * TC], F32))
            dts = [dts_t[:, 0:TC], dts_t[:, TC:2 * TC]]
            dts_guard = [None, None]
            dtx = xcs
            d_tok = None
            for dc in range(DC):
                em.wait("scalar", s_tok, self.vtok["dtb"])
                em.op("scalar", lambda E, o=dt[dc], dc=dc:
                      E.activation(o[:], self.bcS[:], ACT.Abs,
                                   bias=v["dtb"][:, dc:dc + 1], scale=v["dtw"][:, dc:dc + 1]))
                em.op("scalar", lambda E, o=dt[dc]:
                      E.activation(o[:], o[:], ACT.Exp, bias=0.0, scale=-1.0))
                em.op("scalar", lambda E, o=dt[dc]:
                      E.activation(o[:], o[:], ACT.Ln,
                                   bias=self.ones_col[:, 0:1], scale=1.0))
                em.wait("scalar", dts_guard[dc % 2])
                s2 = em.op("scalar", lambda E, o=dts[dc % 2], dc=dc:
                           E.activation(o[:], self.bcS[:], ACT.Relu,
                                        bias=v["dtb"][:, dc:dc + 1],
                                        scale=v["dtw"][:, dc:dc + 1]), sig=True)
                em.wait("vector", s2)
                d_tok = em.op("vector", lambda E, o=dt[dc], p=dts[dc % 2]:
                              E.tensor_add(o[:], o[:], p[:]), sig=True)
                dts_guard[dc % 2] = d_tok
            self.tap("dt", dt, d_tok, phase, tc)
            dx_tok = None
            for dc in range(DC):
                em.wait("vector", d_tok, sl_tok, self.xp_done)
                dx_tok = em.op("vector", lambda E, o=dtx[dc], a2=dt[dc]:
                               E.tensor_tensor(o[:], a2[:], o[:], ALU.mult),
                               sig=(dc == DC - 1))
            dA = tset("dA")
            dBx = tset("dBx")
            hh = tset("hh")
            y = tset("y") if not is_p1 else None
            ya_tok = None
            scn_tok = None
            prev_hc_tok = None
            prev_bx = None
            for n in range(NST):
                em.wait("sync", prev_bx, prev_hc_tok)
                b_tok = em.dma("sync", self.bcB[:], hrow(n))
                c_tok = em.dma("sync", self.bcC[:], hrow(NST + n)) if not is_p1 else None
                aw = None
                for dc in range(DC):
                    em.wait("scalar", d_tok, self.A_tok, scn_tok)
                    aw = em.op("scalar", lambda E, o=dA[dc], i=dt[dc], dc=dc, n=n:
                               E.activation(o[:], i[:], ACT.Exp, bias=0.0,
                                            scale=self.Aneg[:, dc * NST + n:dc * NST + n + 1]),
                               sig=(dc == DC - 1))
                bx = None
                for dc in range(DC):
                    em.wait("vector", b_tok, dx_tok, prev_hc_tok or scn_tok, ya_tok)
                    bx = em.op("vector", lambda E, o=dBx[dc], a2=dtx[dc]:
                               E.tensor_tensor(o[:], a2[:], self.bcB[:], ALU.mult),
                               sig=(dc == DC - 1))
                prev_bx = bx
                em.wait("vector", aw, bx, self.state_tok)
                for dc in range(DC):
                    col = dc * NST + n
                    t = em.op("vector", lambda E, o=hh[dc], A=dA[dc], Bb=dBx[dc], col=col:
                              E.tensor_tensor_scan(o[:], A[:], Bb[:],
                                                   self.st1[:, col:col + 1],
                                                   ALU.mult, ALU.add), sig=True)
                    self.keeper(t)
                for dc in range(DC):
                    col = dc * NST + n
                    scn_tok = em.op("vector", lambda E, o=hh[dc], col=col:
                                    E.tensor_copy(self.st1[:, col:col + 1], o[:, TC - 1:TC]),
                                    sig=(dc == DC - 1))
                if not is_p1:
                    hc_tok = None
                    for dc in range(DC):
                        em.wait("vector", scn_tok, c_tok, ya_tok)
                        hc_tok = em.op("vector", lambda E, o=dBx[dc], a2=hh[dc]:
                                       E.tensor_tensor(o[:], a2[:], self.bcC[:], ALU.mult),
                                       sig=(dc == DC - 1))
                    prev_hc_tok = hc_tok
                    for dc in range(DC):
                        em.wait("gpsimd", hc_tok)
                        if n == 0:
                            ya_tok = em.op("gpsimd", lambda E, o=y[dc], a2=dBx[dc]:
                                           E.tensor_copy(o[:], a2[:]),
                                           sig=(dc == DC - 1))
                        else:
                            ya_tok = em.op("gpsimd", lambda E, o=y[dc], a2=dBx[dc]:
                                           E.tensor_add(o[:], o[:], a2[:]),
                                           sig=(dc == DC - 1))
            self.scan1_tok = scn_tok
            setattr(self, f"_ssm_guard{slot}", scn_tok if is_p1 else ya_tok)
            if is_p1:
                self.rs_tok = rs_tok
                return
            self.tap("ymam", y, ya_tok, phase, tc)
            zs_tok = None
            for dc in range(DC):
                em.wait("scalar", zt)
                zs_tok = em.op("scalar", lambda E, o=z[dc]:
                               E.activation(o[:], o[:], ACT.Silu, bias=0.0, scale=1.0),
                               sig=(dc == DC - 1))
            ym_tok = None
            for dc in range(DC):
                em.wait("vector", zs_tok, ya_tok)
                ym_tok = em.op("vector", lambda E, o=y[dc], a2=z[dc]:
                               E.tensor_tensor(o[:], o[:], a2[:], ALU.mult),
                               sig=(dc == DC - 1))
            y3 = dt
            sqt = dA
            ysh, ysl, ysp_tok = self.split(tset, "ysp", y, ym_tok)
            e_tok = None
            for mc in range(DC):
                bank = mc % 2
                mm = self.mm_block3("wop", mc, DC, ysh, ysl, ysp_tok, bank)
                em.wait("scalar", mm)
                ev = em.op("scalar", lambda E, o=y3[mc], pb=self.pb[bank]:
                           E.activation(o[:], pb[:], ACT.Identity, bias=0.0, scale=1.0),
                           sig=True)
                self.pb_free[bank] = ev
                em.wait("vector", ev)
                e_tok = em.op("vector", lambda E, o=y3[mc], r2=rs[mc]:
                              E.tensor_add(o[:], o[:], r2[:]), sig=(mc == DC - 1))
            self.rs_tok = self.layernorm(y3, e_tok, rs, sqt)
            self.tap("ln3", rs, self.rs_tok, phase, tc)


def _tokq(self, eng, kind="c"):
    name = f"{eng}_{kind}"
    return (name, self.counts[name])


Em.tokq = _tokq



def _wblocks(wt):
    K, M = wt.shape
    return np.ascontiguousarray(
        wt.reshape(K // 128, 128, M // 128, 128).transpose(2, 1, 0, 3)
        .reshape(M // 128, 128, (K // 128) * 128))


def _feat(vec, cols=1):
    return np.ascontiguousarray(np.asarray(vec).reshape(DC, 128, cols))


def stage_inputs(inputs):
    x = np.asarray(inputs["L0_attn_x"])
    base = {}
    base["wr"] = _wblocks(np.ascontiguousarray(np.asarray(inputs["L0_attn_Wr_w"]).T))
    base["wi"] = _wblocks(np.ascontiguousarray(np.asarray(inputs["L0_attn_Wi_w"]).T))
    base["wo0"] = _wblocks(np.ascontiguousarray(np.asarray(inputs["L0_attn_out_w"]).T))
    base["wr_b"] = _feat(inputs["L0_attn_Wr_b"])
    base["wi_b"] = _feat(inputs["L0_attn_Wi_b"])
    base["cw0"] = _feat(np.asarray(inputs["L0_attn_conv_weight"]).reshape(D, KER), KER)
    base["cb0"] = _feat(inputs["L0_attn_conv_bias"])
    base["log_a"] = _feat(inputs["L0_attn_log_a"])
    base["norm_w"] = _feat(inputs["L0_attn_norm_w"])
    base["w1_0"] = _wblocks(np.asarray(inputs["L0_ffn_W1"]))
    base["b1_0"] = np.ascontiguousarray(
        np.asarray(inputs["L0_ffn_b1"]).reshape(HC, 128).T)
    base["w2_0"] = _wblocks(np.asarray(inputs["L0_ffn_W2"]))
    base["b2_0"] = _feat(inputs["L0_ffn_b2"])
    base["wip"] = _wblocks(np.ascontiguousarray(np.asarray(inputs["L1_attn_in_proj_w"]).T))
    base["cw1"] = _feat(np.asarray(inputs["L1_attn_conv_w"]).reshape(D, KER), KER)
    base["cb1"] = _feat(inputs["L1_attn_conv_b"])
    base["wxp"] = np.ascontiguousarray(
        np.asarray(inputs["L1_attn_x_proj_w"]).T.reshape(DC, 128, 33)
        .transpose(1, 0, 2).reshape(128, DC * 33))
    base["dtw"] = _feat(np.asarray(inputs["L1_attn_dt_proj_w"]).reshape(D))
    base["dtb"] = _feat(inputs["L1_attn_dt_proj_b"])
    base["alog"] = np.ascontiguousarray(np.asarray(inputs["L1_attn_A_log"]).reshape(DC, 128, NST))
    base["wop"] = _wblocks(np.ascontiguousarray(np.asarray(inputs["L1_attn_out_proj_w"]).T))
    base["w1_1"] = _wblocks(np.asarray(inputs["L1_ffn_w1"]))
    base["b1_1"] = np.ascontiguousarray(
        np.asarray(inputs["L1_ffn_b1"]).reshape(HC, 128).T)
    base["w2_1"] = _wblocks(np.asarray(inputs["L1_ffn_w2"]))
    base["b2_1"] = _feat(inputs["L1_ffn_b2"])

    consts = np.zeros((128, 4), np.float32)
    consts[:, 0] = 1.0
    consts[:, 1] = np.float32(2 ** 29)
    base["consts"] = consts
    base["bigrow"] = np.full((1, 512), np.float32(2 ** 29), np.float32)
    base["log_a_t"] = np.ascontiguousarray(
        np.asarray(inputs["L0_attn_log_a"]).reshape(DC, 128).T.astype(np.float32))

    def xslice(b, lo, hi):
        out = np.zeros((D, hi - lo + 3), np.float32)
        s = max(lo - 3, 0)
        out[:, 3 - (lo - s):] = x[b, s:hi].T
        return np.ascontiguousarray(out.reshape(DC, 128, hi - lo + 3))

    maps = []
    for c in range(8):
        b, h = c & 3, c >> 2
        m = dict(base)
        m["x_pre"] = xslice(b, 0, TP)
        m["x_own"] = m["x_pre"] if h == 0 else xslice(b, TP, 2 * TP)
        m["flag"] = np.full((128, 1), float(h), np.float32)
        maps.append(m)
    return maps


_PROG_CACHE = {}


def get_prog(dbg_names=(), dbg_phase=2):
    key = (tuple(dbg_names), dbg_phase)
    if key not in _PROG_CACHE:
        _PROG_CACHE[key] = Prog(dbg_names, dbg_phase).build()
    return _PROG_CACHE[key]


def kernel(**inputs):
    maps = stage_inputs(inputs)
    nc = get_prog()
    res = run_bass_kernel_spmd(nc, maps, core_ids=list(range(8)))
    out = np.empty((BATCH, L, D), np.float32)
    for c in range(8):
        b, h = c & 3, c >> 2
        o = res.results[c]["out"]
        out[b, h * TP:(h + 1) * TP, :] = o.reshape(D, TP).T
    return out
